# version 10
# baseline (speedup 1.0000x reference)
import numpy as np
import concourse.bass as bass
import concourse.mybir as mybir

F32 = mybir.dt.float32
AF = mybir.ActivationFunctionType
ALU = mybir.AluOpType

ENGINES = ("sync", "tensor", "vector", "scalar", "gpsimd")


class _Op:
    __slots__ = ("idx", "engine", "emit", "deps", "is_dma", "dma_stream",
                 "sig_idx", "waits", "tag", "drain_before", "raw_deps")
    def __init__(self, idx, engine, emit, deps, is_dma, dma_stream, tag):
        self.idx = idx
        self.engine = engine
        self.emit = emit
        self.deps = deps
        self.is_dma = is_dma
        self.dma_stream = dma_stream
        self.sig_idx = None
        self.waits = None
        self.tag = tag
        self.drain_before = False
        self.raw_deps = set()


class Builder:
    def __init__(self):
        self.ops = []
        self.bufs = {}
        self.dma_streams = {}

    def _buf(self, name):
        if name not in self.bufs:
            self.bufs[name] = {"w": None, "r": []}
        return self.bufs[name]

    def op(self, engine, emit, reads=(), writes=(), tag="", dma_stream=None):
        assert engine in ENGINES, engine
        deps = set()
        raw = set()
        for b in reads:
            e = self._buf(b)
            if e["w"] is not None:
                deps.add(e["w"])
                raw.add(e["w"])
        for b in writes:
            e = self._buf(b)
            if e["w"] is not None:
                deps.add(e["w"])
            deps.update(e["r"])
        o = _Op(len(self.ops), engine, emit, deps, dma_stream is not None, dma_stream, tag)
        o.raw_deps = raw
        self.ops.append(o)
        for b in reads:
            self._buf(b)["r"].append(o)
        for b in writes:
            self.bufs[b] = {"w": o, "r": []}
        return o

    def dma(self, out_ap, in_ap, stream, reads=(), writes=(), tag="", engine="sync"):
        def emit(eng):
            return eng.dma_start(out=out_ap, in_=in_ap)
        return self.op(engine, emit, reads=reads, writes=writes, tag=tag,
                       dma_stream=stream)

    def finalize(self, nc, block, final_engine="sync"):
        needs_sig = set()
        for o in self.ops:
            for d in o.deps:
                if d.is_dma or d.engine != o.engine:
                    needs_sig.add(d)
        for o in self.ops:
            if o.is_dma:
                needs_sig.add(o)
        eng_count = {e: 0 for e in ENGINES}
        stream_count = {}
        for o in self.ops:
            if o not in needs_sig:
                continue
            if o.is_dma:
                c = stream_count.get(o.dma_stream, 0) + 1
                stream_count[o.dma_stream] = c
                o.sig_idx = c * 16
            else:
                c = eng_count[o.engine] + 1
                eng_count[o.engine] = c
                o.sig_idx = c
        watermark = {}
        for o in self.ops:
            waits = {}
            for d in o.deps:
                if d.is_dma:
                    key = ("dma", d.dma_stream)
                elif d.engine != o.engine:
                    key = ("eng", d.engine)
                else:
                    continue
                assert d.sig_idx is not None
                if waits.get(key, 0) < d.sig_idx:
                    waits[key] = d.sig_idx
            lst = []
            for key, v in waits.items():
                wm = watermark.get((o.engine, key), 0)
                if v > wm:
                    watermark[(o.engine, key)] = v
                    lst.append((key, v))
            o.waits = lst
        last_drain = -1
        for o in self.ops:
            if o.engine != "vector" or o.is_dma:
                continue
            mx = -1
            for d in o.raw_deps:
                if (not d.is_dma) and d.engine == "vector":
                    mx = max(mx, d.idx)
            if mx > last_drain:
                o.drain_before = True
                last_drain = o.idx
        self.stream_totals = stream_count
        return eng_count, stream_count

    def emit_engine(self, engname, engproxy, sems):
        for o in self.ops:
            if o.engine != engname:
                continue
            if o.drain_before:
                engproxy.drain()
            for key, v in o.waits:
                engproxy.wait_ge(sems[key], v)
            inst = o.emit(engproxy)
            if o.sig_idx is not None:
                if o.is_dma:
                    inst.then_inc(sems[("dma", o.dma_stream)], 16)
                else:
                    inst.then_inc(sems[("eng", engname)], 1)


"""Megakernel program builder v2: 2-way pipelined attention instances,
overlapped weight streaming. Residual token-major; delta rule chunk=128 via
exact (I-B)^-1 = prod(I+B^{2^k}); all matmuls fp32."""
import numpy as np
from contextlib import ExitStack
import concourse.bass as bass
import concourse.mybir as mybir
from concourse.bass_types import AP

F32 = mybir.dt.float32
AF = mybir.ActivationFunctionType
ALU = mybir.AluOpType
AX = mybir.AxisListType

TILE = 256
G = 2
H = 8
FC = 8
D = 1024
NE = 64
NLEV = 7


def host_constants():
    i_idx = np.arange(128)[:, None]
    j_idx = np.arange(128)[None, :]
    return {
        "ident": np.eye(128, dtype=np.float32),
        "m_low_neg": np.where(i_idx > j_idx, -1.0, 0.0).astype(np.float32),
        "m_up_neg": np.where(i_idx < j_idx, -1.0, 0.0).astype(np.float32),
        "m_upincl": np.where(i_idx <= j_idx, 1.0, 0.0).astype(np.float32),
        "ones_col": np.ones((128, 1), np.float32),
        "ones_row": np.ones((1, 128), np.float32),
    }


def host_inputs(full, core):
    x = np.asarray(full["L0_attn_hidden_states"])
    xs = np.ascontiguousarray(x[2 * core:2 * core + 2].reshape(4096, D))
    conv = np.stack([np.asarray(full["L0_attn_conv_q_w"]),
                     np.asarray(full["L0_attn_conv_k_w"]),
                     np.asarray(full["L0_attn_conv_v_w"])])
    conv = conv.reshape(3, FC, 128, 4).transpose(2, 0, 1, 3).reshape(128, 96)
    bmix = np.concatenate([np.asarray(full["L0_attn_b_w"]),
                           np.asarray(full["L0_attn_mix_w"])], axis=1)
    m = {
        "x_in": xs,
        "wq": np.asarray(full["L0_attn_q_w"]),
        "wk": np.asarray(full["L0_attn_k_w"]),
        "wv": np.asarray(full["L0_attn_v_w"]),
        "wo": np.asarray(full["L0_attn_o_w"]),
        "convw": np.ascontiguousarray(conv),
        "bmix_w": np.ascontiguousarray(bmix),
        "mix_b": np.asarray(full["L0_attn_mix_b"]).reshape(1, 8),
        "mix_bias": np.asarray(full["L0_attn_mix_bias"]).reshape(1, 8),
        "onorm": np.asarray(full["L0_attn_o_norm_w"]).reshape(128, 1),
        "sel_r": np.ascontiguousarray(np.asarray(full["L0_ffn_expert_sel"]).T),
        "keys": np.asarray(full["L0_ffn_keys"]),
        "values": np.asarray(full["L0_ffn_values"]),
    }
    m.update(host_constants())
    return {k: np.ascontiguousarray(v, dtype=np.float32) for k, v in m.items()}


def build(NT=16, tiles_per_seq=8):
    nc = bass.Bass("TRN2", target_bir_lowering=False)
    B = Builder()
    ctx = ExitStack()

    dram_in = {}
    def din(name, shape):
        dram_in[name] = nc.dram_tensor(name, shape, F32, kind="ExternalInput").ap()
        return dram_in[name]

    x_in = din("x_in", [4096, D])
    wq = din("wq", [D, D]); wk = din("wk", [D, D]); wv = din("wv", [D, D])
    wo = din("wo", [D, D])
    convw_d = din("convw", [128, 96])
    bmix_d = din("bmix_w", [D, 16])
    mixb_d = din("mix_b", [1, 8]); mixbias_d = din("mix_bias", [1, 8])
    onorm_d = din("onorm", [128, 1])
    sel_d = din("sel_r", [D, NE])
    keys_d = din("keys", [NE, D, 128])
    values_d = din("values", [NE, 128, D])
    for cn in ("ident", "m_low_neg", "m_up_neg", "m_upincl"):
        din(cn, [128, 128])
    din("ones_col", [128, 1]); din("ones_row", [1, 128])

    out_d = nc.dram_tensor("out", [4096, D], F32, kind="ExternalOutput").ap()
    gdram = nc.dram_tensor("g_dram", [NE, TILE], F32, kind="Internal").ap()

    sbufs = {}
    def sb(name, shape):
        h = ctx.enter_context(nc.sbuf_tensor(name, shape, F32))
        sbufs[name] = h
        return h

    wbuf = [sb(f"wbuf{i}", [128, FC * 512]) for i in range(2)]
    wo_sb = [sb(f"wo{h}", [128, D]) for h in range(H)]
    sel_sb = sb("sel_sb", [128, FC * NE])
    ke_sb = [sb(f"ke{i}", [128, D]) for i in range(2)]
    ve_sb = [sb(f"ve{i}", [128, D]) for i in range(2)]
    x_tok = [sb(f"x_tok{g}", [128, D]) for g in range(G)]
    lnt = [sb(f"lnt{g}", [128, D]) for g in range(G)]
    ln1_fm = [sb(f"lnf{c}", [128, TILE]) for c in range(FC)]
    ln2_fm = ln1_fm
    pre = {p: [sb(f"{p}_pre{c}", [128, TILE + 3]) for c in range(FC)] for p in "qkv"}
    fm = {p: [pre[p][c][:, 3:3 + TILE] for c in range(FC)] for p in "qkv"}
    tails = {p: sb(f"{p}_tail", [128, 3 * FC]) for p in "qkv"}
    cv = {p: sb(f"cv_{p}", [128, TILE]) for p in "qkv"}
    convw = sb("convw_sb", [128, 96])
    bmix_sb = sb("bmix_sb", [128, FC * 16])
    biasrow = sb("biasrow", [1, 16])
    tmp8 = sb("tmp8", [1, 8])
    onorm_sb = sb("onorm_sb", [128, 1])
    bm = [sb(f"bm{g}", [128, 16]) for g in range(G)]
    sqb = [sb(f"sqb{g}", [128, 8]) for g in range(G)]
    S_sb = [sb(f"S{h}", [128, 128]) for h in range(H)]
    consts = {cn: sb(cn + "_sb", list(np.shape(host_constants()[cn])))
              for cn in ("ident", "m_low_neg", "m_up_neg", "m_upincl",
                          "ones_col", "ones_row")}
    inst = []
    for pn in range(4):
        dd = {}
        names = [("k_tok", [128, 128]), ("uw0", [128, 256]),
                 ("ktf", [128, 128]),
                 ("attnT", [128, 128]), ("u_p", [128, 128]),
                 ("Ma", [128, 128]), ("sq_i", [128, 128]),
                 ("kss", [128, 1]), ("kss2", [128, 1]), ("rsqk", [128, 1]),
                 ("nwa", [128, 1]), ("nwb", [128, 1])]
        for nm, shp in names:
            dd[nm] = sb(f"{nm}_{pn}", shp)
        dd["P"] = [sb(f"P{k}_{pn}", [128, 128]) for k in range(2)]
        dd["PT"] = [sb(f"PT{k}_{pn}", [128, 128]) for k in range(2)]
        dd["_p"] = pn
        inst.append(dd)
    nw1 = sb("nw1", [128, 1]); nw2 = sb("nw2", [128, 1])
    po = []
    for j in range(3):
        po.append({nm: sb(f"po_{nm}{j}", [128, 1]) for nm in ("oss", "oss2", "rms", "nwa", "nwb")})
        po[j]["sq"] = sb(f"po_sq{j}", [128, 128])
        po[j]["ofm"] = sb(f"po_ofm{j}", [128, 128])
    qhf_t = [sb(f"qhf_t{h}", [128, TILE]) for h in range(H)]
    v_tok_t = [sb(f"v_tok_t{h}", [128, 128]) for h in range(H)]
    o_raw = [sb(f"o_raw{h}", [128, 128]) for h in range(H)]
    qsq_t = [sb(f"qsq_t{h}", [128, TILE]) for h in range(2)]
    qrow_t = [sb(f"qrow_t{h}", [1, TILE]) for h in range(2)]
    rsqq_t = [sb(f"rsqq_t{h}", [1, TILE]) for h in range(2)]
    nra_t = [sb(f"nra_t{h}", [1, TILE]) for h in range(2)]
    ml = [sb(f"ml{g}", [128, 1]) for g in range(G)]
    rl = [sb(f"rl{g}", [128, 1]) for g in range(G)]
    ssl = sb("ssl", [128, 1]); v1l = sb("v1l", [128, 1]); m2l = sb("m2l", [128, 1])
    sgate = [sb(f"sgate{g}", [128, NE]) for g in range(G)]
    top8 = sb("top8", [128, 8]); gmask = sb("gmask", [128, NE])
    g_fm = sb("g_fm", [64, TILE])
    sc_sb = [sb(f"sc{i}", [128, TILE]) for i in range(2)]
    gb_sb = [sb(f"gb{i}", [128, 2 * TILE]) for i in range(2)]
    g8 = [sb(f"g8_{i}", [1, 2 * TILE]) for i in range(2)]

    psum = [ctx.enter_context(nc.psum_tensor(f"pb{i}", [128, 512], F32)) for i in range(8)]
    PB = [p.ap() for p in psum]

    AOB = [4, 5]
    OUT2B = {(0, 0): 4, (0, 1): 5, (1, 0): 0, (1, 1): 1}
    _rot = {"tp": 0, "mm": 0, "small": 0}
    def rot(kind):
        lst = {"tp": [0, 1], "mm": [2, 3], "small": [6, 7]}[kind]
        i = _rot[kind] % len(lst); _rot[kind] += 1
        return lst[i]

    def dma(out_ap, in_ap, stream, reads=(), writes=(), tag=""):
        return B.dma(out_ap, in_ap, stream, reads=reads, writes=writes, tag=tag)

    def mm(bank, out_ap, lhsT, rhs, reads, start=True, stop=True, tag=""):
        def e(eng, out_ap=out_ap, lhsT=lhsT, rhs=rhs, start=start, stop=stop):
            return eng.matmul(out_ap, lhsT, rhs, start=start, stop=stop)
        return B.op("tensor", e, reads=list(reads), writes=[f"pb{bank}"], tag=tag)

    def tr(bank, out_ap, in_ap, reads, tag=""):
        def e(eng, out_ap=out_ap, in_ap=in_ap):
            return eng.transpose(out_ap, in_ap, consts["ident"][:, :])
        return B.op("tensor", e, reads=list(reads) + ["ident_sb"], writes=[f"pb{bank}"], tag=tag)

    def vop(emit, reads, writes, tag=""):
        return B.op("vector", emit, reads=reads, writes=writes, tag=tag)
    def sop(emit, reads, writes, tag=""):
        return B.op("scalar", emit, reads=reads, writes=writes, tag=tag)
    def gop(emit, reads, writes, tag=""):
        return B.op("gpsimd", emit, reads=reads, writes=writes, tag=tag)

    def emit_rsqrt(v_ap, vname, out_ap, outname, t1_ap, t1n, t2_ap, t2n, tag):
        vop(lambda e: e.reciprocal(t1_ap, v_ap), reads=[vname], writes=[t1n], tag=tag + "_rc")
        sop(lambda e: e.activation(t2_ap, t1_ap, AF.Sqrt), reads=[t1n], writes=[t2n], tag=tag + "_sq")
        gop(lambda e: e.tensor_mul(t1_ap, t2_ap, t2_ap), reads=[t2n], writes=[t1n], tag=tag + "_n1")
        gop(lambda e: e.tensor_mul(t1_ap, t1_ap, v_ap), reads=[t1n, vname], writes=[t1n], tag=tag + "_n2")
        gop(lambda e: e.tensor_scalar(t1_ap, t1_ap, -0.5, 1.5, ALU.mult, ALU.add), reads=[t1n], writes=[t1n], tag=tag + "_n3")
        gop(lambda e: e.tensor_mul(out_ap, t2_ap, t1_ap), reads=[t1n, t2n], writes=[outname], tag=tag + "_n4")

    def ln_token_major(g, t):
        xn, lname = f"x_tok{g}", f"lnt{g}"
        mname, rname = f"m{g}", f"r{g}"
        x_ap = x_tok[g][:, :]; ln_ap = lnt[g][:, :]
        m_t, r_t = ml[g], rl[g]
        vop(lambda e: e.reduce_sum(m_t[:, :], x_ap, axis=AX.X), reads=[xn], writes=[mname], tag=f"lnsum{t}{g}")
        vop(lambda e: e.tensor_scalar(m_t[:, :], m_t[:, :], 1.0 / 1024.0, None, ALU.mult),
            reads=[mname], writes=[mname], tag=f"lnm{t}{g}")
        sop(lambda e: e.activation(ln_ap, x_ap, AF.Square, accum_out=ssl[:, :]),
            reads=[xn], writes=["ssl", lname], tag=f"lnsq{t}{g}")
        vop(lambda e: e.tensor_scalar(v1l[:, :], ssl[:, :], 1.0 / 1024.0, 1e-5, ALU.mult, ALU.add),
            reads=["ssl"], writes=["v1l"], tag=f"lnv1{t}{g}")
        vop(lambda e: e.tensor_mul(m2l[:, :], m_t[:, :], m_t[:, :]), reads=[mname], writes=["m2l"], tag=f"lnm2{t}{g}")
        vop(lambda e: e.tensor_sub(v1l[:, :], v1l[:, :], m2l[:, :]), reads=["v1l", "m2l"], writes=["v1l"], tag=f"lnv{t}{g}")
        emit_rsqrt(v1l[:, :], "v1l", r_t[:, :], rname, nw1[:, :], "nw1", nw2[:, :], "nw2", f"lnr{t}{g}")
        vop(lambda e: e.tensor_scalar(ln_ap, x_ap, m_t[:, :], r_t[:, :], ALU.subtract, ALU.mult),
            reads=[xn, mname, rname], writes=[lname], tag=f"lnn{t}{g}")

    def attn_instance(t, g, h, tail_hook=None):
        d = inst[h % 4]
        pn = d["_p"]
        def N(nm):
            return f"{nm}_{pn}"
        mmb = {0: [0, 2], 1: [1, 3], 2: [6, 7], 3: [4, 5]}[pn]
        mi = [0]
        def mrot():
            b = mmb[mi[0] % 2]; mi[0] += 1
            return b
        qs = fm["q"][h][:, g * 128:(g + 1) * 128]
        ks = fm["k"][h][:, g * 128:(g + 1) * 128]
        vs = fm["v"][h][:, g * 128:(g + 1) * 128]
        kn, qn, vn = f"k_pre{h}", f"q_pre{h}", f"v_pre{h}"
        k_tok = d["k_tok"]; uw0 = d["uw0"]; uw_s = d["uw0"]
        ktf = d["ktf"]; wtf = d["ktf"]; attnT = d["attnT"]; u_p = d["u_p"]
        qhf = qhf_t[h][:, g * 128:(g + 1) * 128]
        Ma = d["Ma"]
        P = d["P"]; PT = d["PT"]; sq_i = d["sq_i"]
        sqb_col = sqb[g][:, h:h + 1]
        gcol = bm[g][:, 8 + h:9 + h]
        Sn = f"S{h}"

        b_ = mrot()
        tr(b_, PB[b_][:, 0:128], ks, reads=[kn], tag=f"kT{t}{g}{h}")
        sop(lambda e, b_=b_: e.copy(k_tok[:, :], PB[b_][:, 0:128]), reads=[f"pb{b_}"], writes=[N("k_tok")], tag="kTe")
        sop(lambda e: e.activation(sq_i[:, :], k_tok[:, :], AF.Square, accum_out=d["kss"][:, :]),
            reads=[N("k_tok")], writes=[N("kss"), N("sq_i")], tag="ksq")
        vop(lambda e: e.tensor_scalar(d["kss2"][:, :], d["kss"][:, :], 1e-12, None, ALU.add),
            reads=[N("kss")], writes=[N("kss2")], tag="keps")
        emit_rsqrt(d["kss2"][:, :], N("kss2"), d["rsqk"][:, :], N("rsqk"),
                   d["nwa"][:, :], N("nwa"), d["nwb"][:, :], N("nwb"), f"krsq{pn}")
        yield
        b_ = mrot()
        tr(b_, PB[b_][:, 0:128], vs, reads=[vn], tag=f"vT{t}{g}{h}")
        sop(lambda e, b_=b_: e.copy(v_tok_t[h][:, :], PB[b_][:, 0:128]), reads=[f"pb{b_}"], writes=[f"v_tok_t{h}"], tag="vTe")
        sop(lambda e: e.activation(k_tok[:, :], k_tok[:, :], AF.Identity, scale=d["rsqk"][:, :]),
            reads=[N("k_tok"), N("rsqk")], writes=[N("k_tok")], tag="khat")
        sop(lambda e: e.activation(uw0[:, 128:256], k_tok[:, :], AF.Identity, scale=sqb_col),
            reads=[N("k_tok"), f"sqb{g}"], writes=[N("uw0")], tag="ktil")
        sop(lambda e: e.activation(uw0[:, 0:128], v_tok_t[h][:, :], AF.Identity, scale=sqb_col),
            reads=[f"v_tok_t{h}", f"sqb{g}"], writes=[N("uw0")], tag="vtil")
        yield
        b_ = mrot()
        tr(b_, PB[b_][:, 0:128], uw0[:, 128:256], reads=[N("uw0")], tag="ktfT")
        sop(lambda e, b_=b_: e.copy(ktf[:, :], PB[b_][:, 0:128]), reads=[f"pb{b_}"], writes=[N("ktf")], tag="ktfE")
        yield
        b0 = mrot()
        mm(b0, PB[b0][:, 0:128], ktf[:, :], ktf[:, :], reads=[N("ktf")], tag="Gt")
        def e(eng, b0=b0):
            return eng.tensor_tensor(P[0][:, :], PB[b0][:, 0:128], consts["m_low_neg"][:, :], ALU.mult)
        vop(e, reads=[f"pb{b0}", "m_low_neg_sb"], writes=[N("P0")], tag="Bev")
        def e(eng, b0=b0):
            return eng.tensor_tensor(PT[0][:, :], PB[b0][:, 0:128], consts["m_up_neg"][:, :], ALU.mult)
        vop(e, reads=[f"pb{b0}", "m_up_neg_sb"], writes=[N("PT0")], tag="BTev")
        vop(lambda e: e.tensor_add(Ma[:, :], PT[0][:, :], consts["ident"][:, :]),
            reads=[N("PT0"), "ident_sb"], writes=[N("Ma")], tag="M1")
        yield
        Man = N("Ma")
        for k in range(1, NLEV):
            sl_ = k % 2; slp = (k - 1) % 2
            bk = mrot()
            mm(bk, PB[bk][:, 0:128], PT[slp][:, :], P[slp][:, :],
               reads=[N(f"PT{slp}"), N(f"P{slp}")], tag=f"sqP{k}")
            ep = sop if k % 2 == 0 else vop
            def e(eng, sl_=sl_, bk=bk):
                if hasattr(eng, "copy"):
                    return eng.copy(P[sl_][:, :], PB[bk][:, 0:128])
                return eng.tensor_copy(P[sl_][:, :], PB[bk][:, 0:128])
            ep(e, reads=[f"pb{bk}"], writes=[N(f"P{sl_}")], tag=f"sqPe{k}")
            if k < NLEV - 1:
                bk2 = mrot()
                mm(bk2, PB[bk2][:, 0:128], P[slp][:, :], PT[slp][:, :],
                   reads=[N(f"P{slp}"), N(f"PT{slp}")], tag=f"sqT{k}")
                ep2 = vop if k % 2 == 0 else sop
                def e(eng, sl_=sl_, bk2=bk2):
                    if hasattr(eng, "copy"):
                        return eng.copy(PT[sl_][:, :], PB[bk2][:, 0:128])
                    return eng.tensor_copy(PT[sl_][:, :], PB[bk2][:, 0:128])
                ep2(e, reads=[f"pb{bk2}"], writes=[N(f"PT{sl_}")], tag=f"sqTe{k}")
            bk3 = mrot()
            mm(bk3, PB[bk3][:, 0:128], P[sl_][:, :], Ma[:, :],
               reads=[N(f"P{sl_}"), Man], tag=f"ap{k}")
            def e(eng, bk3=bk3):
                return eng.tensor_add(Ma[:, :], Ma[:, :], PB[bk3][:, 0:128])
            vop(e, reads=[f"pb{bk3}", Man], writes=[Man], tag=f"apE{k}")
            yield
        bk = mrot()
        mm(bk, PB[bk][:, 0:256], Ma[:, :], uw0[:, :], reads=[Man, N("uw0")], tag="uwmm")
        def e(eng, bk=bk):
            return eng.activation(uw_s[:, :], PB[bk][:, 0:256], AF.Identity, scale=sqb_col)
        sop(e, reads=[f"pb{bk}", f"sqb{g}"], writes=[N("uw0")], tag="uwE")
        b_ = mrot()
        tr(b_, PB[b_][:, 0:128], uw_s[:, 128:256], reads=[N("uw0")], tag="wT")
        sop(lambda e, b_=b_: e.copy(wtf[:, :], PB[b_][:, 0:128]), reads=[f"pb{b_}"], writes=[N("ktf")], tag="wTe")
        yield
        bk = mrot()
        mm(bk, PB[bk][:, 0:128], wtf[:, :], S_sb[h][:, :], reads=[N("ktf"), Sn], tag="wS")
        def e(eng, bk=bk):
            return eng.tensor_sub(u_p[:, :], uw_s[:, 0:128], PB[bk][:, 0:128])
        vop(e, reads=[f"pb{bk}", N("uw0")], writes=[N("u_p")], tag="uprime")
        bk = mrot()
        mm(bk, PB[bk][:, 0:128], ks, qhf, reads=[kn, f"qhf_t{h}"], tag="aT")
        def e(eng, bk=bk):
            return eng.scalar_tensor_tensor(attnT[:, :], PB[bk][:, 0:128], d["rsqk"][:, :],
                                            consts["m_upincl"][:, :], ALU.mult, ALU.mult)
        vop(e, reads=[f"pb{bk}", N("rsqk"), "m_upincl_sb"], writes=[N("attnT")], tag="aTe")
        yield
        bk = mrot()
        mm(bk, PB[bk][:, 0:128], qhf, S_sb[h][:, :], reads=[f"qhf_t{h}", Sn], start=True, stop=False, tag="qS")
        mm(bk, PB[bk][:, 0:128], attnT[:, :], u_p[:, :], reads=[N("attnT"), N("u_p")], start=False, stop=True, tag="au")
        def e(eng, bk=bk):
            return eng.tensor_sub(o_raw[h][:, :], PB[bk][:, 0:128], v_tok_t[h][:, :])
        vop(e, reads=[f"pb{bk}", f"v_tok_t{h}"], writes=[f"o_raw{h}"], tag="gsub")
        bk = mrot()
        mm(bk, PB[bk][:, 0:128], k_tok[:, :], u_p[:, :], reads=[N("k_tok"), N("u_p")], tag="Sup")
        def e(eng, bk=bk):
            return eng.tensor_add(S_sb[h][:, :], S_sb[h][:, :], PB[bk][:, 0:128])
        vop(e, reads=[f"pb{bk}", Sn], writes=[Sn], tag="SupE")
        if tail_hook is not None:
            tail_hook(h)

    dma(convw[:, :], convw_d, "convw", writes=["convw_sb"])
    dma(biasrow[0:1, 8:16], mixb_d, "biasrow", writes=["biasrow"])
    dma(tmp8[:, :], mixbias_d, "tmp8", writes=["tmp8"])
    dma(onorm_sb[:, :], onorm_d, "onorm", writes=["onorm_sb"])
    for cn in consts:
        nm = cn + "_sb"
        dma(sbufs[nm][:, :], dram_in[cn], nm, writes=[nm])
    sel_src = AP(tensor=sel_d.tensor, offset=0, ap=[[NE, 128], [128 * NE, FC], [1, NE]])
    dma(sel_sb[:, :], sel_src, "sel_sb", writes=["sel_sb"])
    for h in range(H):
        dma(wo_sb[h][:, :], wo[h * 128:(h + 1) * 128, :], f"wo{h}", writes=[f"wo{h}"])
    gop(lambda e: e.memset(biasrow[0:1, 0:8], 0.0), reads=["biasrow"], writes=["biasrow"], tag="biasz")
    vop(lambda e: e.tensor_add(biasrow[0:1, 8:16], biasrow[0:1, 8:16], tmp8[0:1, :]),
        reads=["biasrow", "tmp8"], writes=["biasrow"], tag="biasadd")
    for h in range(H):
        def e(eng, h=h):
            return eng.tensor_scalar(wo_sb[h][:, :], wo_sb[h][:, :], onorm_sb[:, :], None, ALU.mult)
        vop(e, reads=[f"wo{h}", "onorm_sb"], writes=[f"wo{h}"], tag="wo_scale")
    bmix_src = AP(tensor=bmix_d.tensor, offset=0, ap=[[16, 128], [128 * 16, FC], [1, 16]])
    dma(bmix_sb[:, :], bmix_src, "bmix_sb", writes=["bmix_sb"])

    for t in range(NT):
        seq_start = (t % tiles_per_seq) == 0
        for g in range(G):
            r0 = t * TILE + g * 128
            dma(x_tok[g][:, :], x_in[r0:r0 + 128, :], f"x_tok{g}",
                writes=[f"x_tok{g}"], tag=f"xin{t}g{g}")
        for g in range(G):
            ln_token_major(g, t)
            for c in range(FC):
                bk = rot("tp")
                tr(bk, PB[bk][:, 0:128], lnt[g][:, c * 128:(c + 1) * 128],
                   reads=[f"lnt{g}"], tag=f"ln1T{t}{g}{c}")
                def e(eng, c=c, g=g, bk=bk):
                    return eng.copy(ln1_fm[c][:, g * 128:(g + 1) * 128], PB[bk][:, 0:128])
                sop(e, reads=[f"pb{bk}"], writes=[f"lnf{c}"], tag=f"ln1E{t}{g}{c}")
        hoist_queue = []
        def _hoist_unit_list():
            units = []
            for h_ in range(H):
                units.append(lambda h_=h_: (_emit_qss(h_), (_emit_qsquare(h_ + 2) if h_ + 2 < H else None)))
                if h_ >= 1:
                    units.append(lambda h_=h_: _emit_qbc(h_ - 1))
            units.append(lambda: _emit_qbc(H - 1))
            return units
        def _emit_qbc(h):
            hp2 = h % 2
            qfm_h = fm["q"][h]
            qnh = f"q_pre{h}"
            bk2 = rot("small")
            mm(bk2, PB[bk2][:, 0:TILE], consts["ones_row"][:, :], rsqq_t[hp2][0:1, :],
               reads=["ones_row_sb", f"rsqq_t{hp2}"], tag=f"qbcH{h}")
            def e(eng, bk2=bk2, qfm_h=qfm_h, h=h):
                return eng.tensor_tensor(qhf_t[h][:, :], qfm_h, PB[bk2][:, 0:TILE], ALU.mult)
            vop(e, reads=[qnh, f"pb{bk2}"], writes=[f"qhf_t{h}"], tag=f"qhatH{h}")
        def _emit_qsquare(h):
            hp2 = h % 2
            qfm_h = fm["q"][h]
            qnh = f"q_pre{h}"
            sop(lambda e, qfm_h=qfm_h, hp2=hp2: e.activation(qsq_t[hp2][:, :], qfm_h, AF.Square),
                reads=[qnh], writes=[f"qsq_t{hp2}"], tag=f"qsqH{h}")
        def _emit_qss(h):
            hp2 = h % 2
            qfm_h = fm["q"][h]
            qnh = f"q_pre{h}"
            bk = rot("small")
            mm(bk, PB[bk][0:1, 0:TILE], consts["ones_col"][:, :], qsq_t[hp2][:, :],
               reads=["ones_col_sb", f"qsq_t{hp2}"], tag=f"qssH{h}")
            def e(eng, bk=bk, hp2=hp2):
                return eng.tensor_scalar(qrow_t[hp2][0:1, :], PB[bk][0:1, 0:TILE], 1e-12, None, ALU.add)
            vop(e, reads=[f"pb{bk}"], writes=[f"qrow_t{hp2}"], tag=f"qepsH{h}")
            emit_rsqrt(qrow_t[hp2][0:1, :], f"qrow_t{hp2}", rsqq_t[hp2][0:1, :], f"rsqq_t{hp2}",
                       nra_t[hp2][0:1, :], f"nra_t{hp2}", rsqq_t[hp2][0:1, :], f"rsqq_t{hp2}", f"qrsqH{h}")
        for pi, (p, wd) in enumerate((("q", wq), ("k", wk), ("v", wv))):
            for c in range(FC):
                if seq_start:
                    gop(lambda e, p=p, c=c: e.memset(pre[p][c][:, 0:3], 0.0),
                        reads=[], writes=[f"{p}_pre{c}"], tag=f"tz{p}{c}")
                else:
                    gop(lambda e, p=p, c=c: e.tensor_copy(pre[p][c][:, 0:3], tails[p][:, c * 3:c * 3 + 3]),
                        reads=[f"{p}_tail"], writes=[f"{p}_pre{c}"], tag=f"tc{p}{c}")
            for hf in range(2):
                wslot = (pi * 2 + hf) % 2
                wsrc = AP(tensor=wd.tensor, offset=hf * 512,
                          ap=[[D, 128], [128 * D, FC], [1, 512]])
                dma(wbuf[wslot][:, :], wsrc, f"wbuf{wslot}", writes=[f"wbuf{wslot}"], tag=f"w{p}{t}h{hf}")
                for oc4 in range(4):
                    oc = hf * 4 + oc4
                    bk = rot("mm")
                    for ic in range(FC):
                        mm(bk, PB[bk][:, 0:TILE],
                           wbuf[wslot][:, ic * 512 + oc4 * 128: ic * 512 + oc4 * 128 + 128],
                           ln1_fm[ic][:, :],
                           reads=[f"wbuf{wslot}", f"lnf{ic}"],
                           start=(ic == 0), stop=(ic == FC - 1), tag=f"prj{p}{oc}i{ic}")
                    def e(eng, p=p, oc=oc, bk=bk):
                        return eng.copy(pre[p][oc][:, 3:3 + TILE], PB[bk][:, 0:TILE])
                    sop(e, reads=[f"pb{bk}"], writes=[f"{p}_pre{oc}"], tag=f"prjE{p}{oc}")
                    gop(lambda e, p=p, oc=oc: e.tensor_copy(tails[p][:, oc * 3:oc * 3 + 3], pre[p][oc][:, TILE:TILE + 3]),
                        reads=[f"{p}_pre{oc}"], writes=[f"{p}_tail"], tag=f"ts{p}{oc}")
                    pcol = {"q": 0, "k": 1, "v": 2}[p]
                    def e0(eng, p=p, oc=oc, pcol=pcol):
                        return eng.tensor_scalar(cv[p][:, :], pre[p][oc][:, 0:TILE],
                                                 convw[:, pcol * 32 + oc * 4: pcol * 32 + oc * 4 + 1], None, ALU.mult)
                    vop(e0, reads=[f"{p}_pre{oc}", "convw_sb"], writes=[f"cv_{p}"], tag=f"cnv{p}{oc}0")
                    for i in (1, 2, 3):
                        def ei(eng, p=p, oc=oc, i=i, pcol=pcol):
                            return eng.scalar_tensor_tensor(cv[p][:, :], pre[p][oc][:, i:i + TILE],
                                                            convw[:, pcol * 32 + oc * 4 + i: pcol * 32 + oc * 4 + i + 1],
                                                            cv[p][:, :], ALU.mult, ALU.add)
                        vop(ei, reads=[f"{p}_pre{oc}", "convw_sb", f"cv_{p}"], writes=[f"cv_{p}"], tag=f"cnv{p}{oc}{i}")
                    def esilu(eng, p=p, oc=oc):
                        return eng.activation(fm[p][oc], cv[p][:, :], AF.Silu)
                    sop(esilu, reads=[f"cv_{p}"], writes=[f"{p}_pre{oc}"], tag=f"silu{p}{oc}")
                    if p == "q":
                        if oc == 6:
                            _emit_qsquare(0)
                        if oc == FC - 1:
                            _emit_qsquare(1)
                            hoist_queue.extend(_hoist_unit_list())
                    elif hoist_queue:
                        hoist_queue.pop(0)()
        for g in range(G):
            bk = rot("small")
            for ic in range(FC):
                mm(bk, PB[bk][:, 0:16], ln1_fm[ic][:, g * 128:(g + 1) * 128],
                   bmix_sb[:, ic * 16:(ic + 1) * 16],
                   reads=[f"lnf{ic}", "bmix_sb"], start=(ic == 0), stop=False, tag=f"bm{g}i{ic}")
            mm(bk, PB[bk][:, 0:16], consts["ones_row"][:, :], biasrow[0:1, :],
               reads=["ones_row_sb", "biasrow"], start=False, stop=True, tag=f"bmb{g}")
            def e(eng, g=g, bk=bk):
                return eng.activation(bm[g][:, :], PB[bk][:, 0:16], AF.Sigmoid)
            sop(e, reads=[f"pb{bk}"], writes=[f"bm{g}"], tag=f"bmS{g}")
            def e2(eng, g=g):
                return eng.activation(sqb[g][:, :], bm[g][:, 0:8], AF.Sqrt)
            sop(e2, reads=[f"bm{g}"], writes=[f"sqb{g}"], tag=f"sqb{g}")
        for g in range(G):
            if seq_start and g == 0:
                for h in range(H):
                    vop(lambda e, h=h: e.memset(S_sb[h][:, :], 0.0),
                        reads=[], writes=[f"S{h}"], tag=f"Sz{h}")
            def post_chain(h):
                d3 = po[h % 3]
                p3 = h % 3
                gcol2 = bm[g][:, 8 + h:9 + h]
                def e(eng, h=h, gcol2=gcol2):
                    return eng.scalar_tensor_tensor(o_raw[h][:, :], o_raw[h][:, :], gcol2,
                                                    v_tok_t[h][:, :], ALU.mult, ALU.add)
                vop(e, reads=[f"o_raw{h}", f"bm{g}", f"v_tok_t{h}"], writes=[f"o_raw{h}"], tag=f"gmix{h}")
                sop(lambda e, h=h, d3=d3: e.activation(d3["sq"][:, :], o_raw[h][:, :], AF.Square, accum_out=d3["oss"][:, :]),
                    reads=[f"o_raw{h}"], writes=[f"po_oss{p3}", f"po_sq{p3}"], tag=f"osq{h}")
                vop(lambda e, d3=d3: e.tensor_scalar(d3["oss2"][:, :], d3["oss"][:, :], 1.0 / 128.0, 1e-5, ALU.mult, ALU.add),
                    reads=[f"po_oss{p3}"], writes=[f"po_oss2{p3}"], tag=f"oeps{h}")
                emit_rsqrt(d3["oss2"][:, :], f"po_oss2{p3}", d3["rms"][:, :], f"po_rms{p3}",
                           d3["nwa"][:, :], f"po_nwa{p3}", d3["nwb"][:, :], f"po_nwb{p3}", f"orsq{h}")
                sop(lambda e, h=h, d3=d3: e.activation(o_raw[h][:, :], o_raw[h][:, :], AF.Identity, scale=d3["rms"][:, :]),
                    reads=[f"o_raw{h}", f"po_rms{p3}"], writes=[f"o_raw{h}"], tag=f"onrm{h}")
            def post_pe(h):
                d3 = po[h % 3]
                p3 = h % 3
                bko = [0, 2, 1, 3][h % 4]
                tr(bko, PB[bko][:, 0:128], o_raw[h][:, :], reads=[f"o_raw{h}"], tag=f"oT{h}")
                sop(lambda e, bko=bko, d3=d3: e.copy(d3["ofm"][:, :], PB[bko][:, 0:128]),
                    reads=[f"pb{bko}"], writes=[f"po_ofm{p3}"], tag=f"oTe{h}")
                for half in range(2):
                    ob = AOB[half]
                    mm(ob, PB[ob][:, 0:512], d3["ofm"][:, :], wo_sb[h][:, half * 512:(half + 1) * 512],
                       reads=[f"po_ofm{p3}", f"wo{h}"], start=(h == 0), stop=(h == H - 1), tag=f"ao{h}_{half}")
            for hp in range(0, H, 4):
                hs = list(range(hp, min(hp + 4, H)))
                gens = [attn_instance(t, g, h_, tail_hook=post_chain) for h_ in hs]
                alive = [True] * len(gens)
                while any(alive):
                    for i_ in range(len(gens)):
                        if alive[i_]:
                            try:
                                next(gens[i_])
                            except StopIteration:
                                alive[i_] = False
            for h in range(H):
                post_pe(h)
            for half in range(2):
                ob = AOB[half]
                def e(eng, g=g, half=half, ob=ob):
                    return eng.tensor_add(x_tok[g][:, half * 512:(half + 1) * 512],
                                          x_tok[g][:, half * 512:(half + 1) * 512],
                                          PB[ob][:, 0:512])
                vop(e, reads=[f"pb{ob}", f"x_tok{g}"], writes=[f"x_tok{g}"], tag=f"res1{g}{half}")
        for g in range(G):
            ln_token_major(g, t)
            for c in range(FC):
                bk = rot("tp")
                tr(bk, PB[bk][:, 0:128], lnt[g][:, c * 128:(c + 1) * 128],
                   reads=[f"lnt{g}"], tag=f"ln2T{t}{g}{c}")
                def e(eng, c=c, g=g, bk=bk):
                    return eng.copy(ln2_fm[c][:, g * 128:(g + 1) * 128], PB[bk][:, 0:128])
                sop(e, reads=[f"pb{bk}"], writes=[f"lnf{c}"], tag=f"ln2E{t}{g}{c}")
        for e_i in (0, 1):
            sl = e_i % 2
            ksrc = AP(tensor=keys_d.tensor, offset=e_i * D * 128,
                      ap=[[128, 128], [128 * 128, FC], [1, 128]])
            dma(ke_sb[sl][:, :], ksrc, f"ke{sl}", writes=[f"ke{sl}"], tag=f"keL{e_i}pre")
            dma(ve_sb[sl][:, :], values_d[e_i, :, :], f"ve{sl}", writes=[f"ve{sl}"], tag=f"veL{e_i}pre")
        for g in range(G):
            bk = rot("small")
            for ic in range(FC):
                mm(bk, PB[bk][:, 0:NE], ln2_fm[ic][:, g * 128:(g + 1) * 128],
                   sel_sb[:, ic * NE:(ic + 1) * NE],
                   reads=[f"lnf{ic}", "sel_sb"], start=(ic == 0), stop=(ic == FC - 1), tag=f"lg{g}{ic}")
            def e(eng, g=g, bk=bk):
                return eng.activation(sgate[g][:, :], PB[bk][:, 0:NE], AF.Sigmoid)
            sop(e, reads=[f"pb{bk}"], writes=[f"sgate{g}"], tag=f"sgS{g}")
            vop(lambda e, g=g: e.max(out=top8[:, :], in_=sgate[g][:, :]),
                reads=[f"sgate{g}"], writes=["top8"], tag=f"top8{g}")
            def e2(eng, g=g):
                return eng.tensor_scalar(gmask[:, :], sgate[g][:, :], top8[:, 3:4], None, ALU.is_ge)
            vop(e2, reads=[f"sgate{g}", "top8"], writes=["gmask"], tag=f"gmask{g}")
            def e3(eng, g=g):
                return eng.tensor_mul(sgate[g][:, :], sgate[g][:, :], gmask[:, :])
            vop(e3, reads=[f"sgate{g}", "gmask"], writes=[f"sgate{g}"], tag=f"gmul{g}")
            bk2 = rot("tp")
            tr(bk2, PB[bk2][0:64, 0:128], sgate[g][:, 0:NE], reads=[f"sgate{g}"], tag=f"gT{g}")
            def e4(eng, g=g, bk2=bk2):
                return eng.copy(g_fm[:, g * 128:(g + 1) * 128], PB[bk2][0:64, 0:128])
            sop(e4, reads=[f"pb{bk2}"], writes=["g_fm"], tag=f"gE{g}")
        dma(gdram, g_fm[:, :], "gdram", reads=["g_fm"], writes=["gdram"], tag=f"gW{t}")
        for e_i in range(NE):
            sl = e_i % 2
            if e_i % 2 == 0:
                gsl = (e_i // 2) % 2
                gsrc = AP(tensor=gdram.tensor, offset=e_i * TILE, ap=[[1, 1], [1, 2 * TILE]])
                dma(g8[gsl][0:1, :], gsrc, f"g8_{gsl}", reads=["gdram"],
                    writes=[f"g8_{gsl}"], tag=f"g8L{e_i}")
            hbk = [2, 3][e_i % 2]
            for ic in range(FC):
                mm(hbk, PB[hbk][:, 0:TILE], ke_sb[sl][:, ic * 128:(ic + 1) * 128],
                   ln2_fm[ic][:, :],
                   reads=[f"ke{sl}", f"lnf{ic}"], start=(ic == 0), stop=(ic == FC - 1),
                   tag=f"h{e_i}i{ic}")
            if e_i % 2 == 0:
                gbk = [6, 7][(e_i // 2) % 2]
                gsl = (e_i // 2) % 2
                gpr = (e_i // 2) % 2
                mm(gbk, PB[gbk][:, 0:2 * TILE], consts["ones_row"][:, :],
                   g8[gsl][0:1, 0:2 * TILE],
                   reads=["ones_row_sb", f"g8_{gsl}"], tag=f"gb{e_i}")
                def eg(eng, gpr=gpr, gbk=gbk):
                    return eng.copy(gb_sb[gpr][:, :], PB[gbk][:, 0:2 * TILE])
                sop(eg, reads=[f"pb{gbk}"], writes=[f"gb{gpr}"], tag=f"gbE{e_i}")
            if e_i + 2 < NE:
                e_n = e_i + 2
                sln = e_n % 2
                ksrc = AP(tensor=keys_d.tensor, offset=e_n * D * 128,
                          ap=[[128, 128], [128 * 128, FC], [1, 128]])
                dma(ke_sb[sln][:, :], ksrc, f"ke{sln}", writes=[f"ke{sln}"], tag=f"keL{e_n}")
            gpr = (e_i // 2) % 2
            def es(eng, sl=sl, hbk=hbk, gpr=gpr, e_i=e_i):
                return eng.scalar_tensor_tensor(sc_sb[sl][:, :], PB[hbk][:, 0:TILE], 0.0,
                                                gb_sb[gpr][:, (e_i % 2) * TILE:(e_i % 2 + 1) * TILE],
                                                ALU.max, ALU.mult)
            vop(es, reads=[f"pb{hbk}", f"gb{gpr}"], writes=[f"sc{sl}"], tag=f"sc{e_i}")
            def emit_out2(e_j):
                slj = e_j % 2
                for g in range(G):
                    for half in range(2):
                        ob = OUT2B[(g, half)]
                        mm(ob, PB[ob][:, 0:512], sc_sb[slj][:, g * 128:(g + 1) * 128],
                           ve_sb[slj][:, half * 512:(half + 1) * 512],
                           reads=[f"sc{slj}", f"ve{slj}"], start=(e_j == 0), stop=(e_j == NE - 1),
                           tag=f"o2_{e_j}g{g}h{half}")
            if e_i >= 1:
                emit_out2(e_i - 1)
            if 2 <= e_i + 1 < NE:
                e_v = e_i + 1
                slv = e_v % 2
                dma(ve_sb[slv][:, :], values_d[e_v, :, :], f"ve{slv}", writes=[f"ve{slv}"], tag=f"veL{e_v}")
            if e_i == NE - 1:
                emit_out2(e_i)
        for g in range(G):
            for half in range(2):
                ob = OUT2B[(g, half)]
                def e(eng, g=g, half=half, ob=ob):
                    return eng.tensor_add(x_tok[g][:, half * 512:(half + 1) * 512],
                                          x_tok[g][:, half * 512:(half + 1) * 512],
                                          PB[ob][:, 0:512])
                vop(e, reads=[f"pb{ob}", f"x_tok{g}"], writes=[f"x_tok{g}"], tag=f"res2{g}{half}")
            r0 = t * TILE + g * 128
            dma(out_d[r0:r0 + 128, :], x_tok[g][:, :], "out",
                reads=[f"x_tok{g}"], tag=f"st{t}{g}")

    return nc, B, ctx


_CACHE = {}

def _get_program():
    if "nc" in _CACHE:
        return _CACHE["nc"]
    nc, B, ctx = build(NT=16)
    B.finalize(nc, None)
    sems = {}
    for e in ENGINES:
        sems[("eng", e)] = ctx.enter_context(nc.semaphore(f"sem_{e}"))
    for s in B.stream_totals:
        sems[("dma", s)] = ctx.enter_context(nc.semaphore(f"dma_{s}"))
    with nc.Block() as block:
        @block.sync
        def _(eng):
            B.emit_engine("sync", eng, sems)
            for s, c in B.stream_totals.items():
                eng.wait_ge(sems[("dma", s)], 16 * c)
        @block.tensor
        def _(eng):
            B.emit_engine("tensor", eng, sems)
        @block.vector
        def _(eng):
            B.emit_engine("vector", eng, sems)
        @block.scalar
        def _(eng):
            B.emit_engine("scalar", eng, sems)
        @block.gpsimd
        def _(eng):
            B.emit_engine("gpsimd", eng, sems)
    ctx.close()
    _CACHE["nc"] = nc
    return nc


def kernel(**inputs):
    from concourse.bass_utils import run_bass_kernel_spmd
    nc = _get_program()
    in_maps = [host_inputs(inputs, core) for core in range(8)]
    res = run_bass_kernel_spmd(nc, in_maps, core_ids=list(range(8)))
    out = np.concatenate([res.results[c]["out"].reshape(2, 2048, 1024) for c in range(8)], axis=0)
    return out


# version 12
# speedup vs baseline: 1.0408x; 1.0408x over previous
import numpy as np
import concourse.bass as bass
import concourse.mybir as mybir

F32 = mybir.dt.float32
AF = mybir.ActivationFunctionType
ALU = mybir.AluOpType

ENGINES = ("sync", "tensor", "vector", "scalar", "gpsimd")


class _Op:
    __slots__ = ("idx", "engine", "emit", "deps", "is_dma", "dma_stream",
                 "sig_idx", "waits", "tag", "drain_before", "raw_deps")
    def __init__(self, idx, engine, emit, deps, is_dma, dma_stream, tag):
        self.idx = idx
        self.engine = engine
        self.emit = emit
        self.deps = deps
        self.is_dma = is_dma
        self.dma_stream = dma_stream
        self.sig_idx = None
        self.waits = None
        self.tag = tag
        self.drain_before = False
        self.raw_deps = set()


class Builder:
    def __init__(self):
        self.ops = []
        self.bufs = {}
        self.dma_streams = {}

    def _buf(self, name):
        if name not in self.bufs:
            self.bufs[name] = {"w": None, "r": []}
        return self.bufs[name]

    def op(self, engine, emit, reads=(), writes=(), tag="", dma_stream=None):
        assert engine in ENGINES, engine
        deps = set()
        raw = set()
        for b in reads:
            e = self._buf(b)
            if e["w"] is not None:
                deps.add(e["w"])
                raw.add(e["w"])
        for b in writes:
            e = self._buf(b)
            if e["w"] is not None:
                deps.add(e["w"])
            deps.update(e["r"])
        o = _Op(len(self.ops), engine, emit, deps, dma_stream is not None, dma_stream, tag)
        o.raw_deps = raw
        self.ops.append(o)
        for b in reads:
            self._buf(b)["r"].append(o)
        for b in writes:
            self.bufs[b] = {"w": o, "r": []}
        return o

    def dma(self, out_ap, in_ap, stream, reads=(), writes=(), tag="", engine="sync"):
        def emit(eng):
            return eng.dma_start(out=out_ap, in_=in_ap)
        return self.op(engine, emit, reads=reads, writes=writes, tag=tag,
                       dma_stream=stream)

    def finalize(self, nc, block, final_engine="sync"):
        needs_sig = set()
        for o in self.ops:
            for d in o.deps:
                if d.is_dma or d.engine != o.engine:
                    needs_sig.add(d)
        for o in self.ops:
            if o.is_dma:
                needs_sig.add(o)
        eng_count = {e: 0 for e in ENGINES}
        stream_count = {}
        for o in self.ops:
            if o not in needs_sig:
                continue
            if o.is_dma:
                c = stream_count.get(o.dma_stream, 0) + 1
                stream_count[o.dma_stream] = c
                o.sig_idx = c * 16
            else:
                c = eng_count[o.engine] + 1
                eng_count[o.engine] = c
                o.sig_idx = c
        watermark = {}
        for o in self.ops:
            waits = {}
            for d in o.deps:
                if d.is_dma:
                    key = ("dma", d.dma_stream)
                elif d.engine != o.engine:
                    key = ("eng", d.engine)
                else:
                    continue
                assert d.sig_idx is not None
                if waits.get(key, 0) < d.sig_idx:
                    waits[key] = d.sig_idx
            lst = []
            for key, v in waits.items():
                wm = watermark.get((o.engine, key), 0)
                if v > wm:
                    watermark[(o.engine, key)] = v
                    lst.append((key, v))
            o.waits = lst
        last_drain = -1
        for o in self.ops:
            if o.engine != "vector" or o.is_dma:
                continue
            mx = -1
            for d in o.deps:
                if (not d.is_dma) and d.engine == "vector":
                    mx = max(mx, d.idx)
            if mx > last_drain:
                o.drain_before = True
                last_drain = o.idx
        self.stream_totals = stream_count
        return eng_count, stream_count

    def emit_engine(self, engname, engproxy, sems):
        for o in self.ops:
            if o.engine != engname:
                continue
            if o.drain_before:
                engproxy.drain()
            for key, v in o.waits:
                engproxy.wait_ge(sems[key], v)
            inst = o.emit(engproxy)
            if o.sig_idx is not None:
                if o.is_dma:
                    inst.then_inc(sems[("dma", o.dma_stream)], 16)
                else:
                    inst.then_inc(sems[("eng", engname)], 1)


"""Megakernel program builder v2: 2-way pipelined attention instances,
overlapped weight streaming. Residual token-major; delta rule chunk=128 via
exact (I-B)^-1 = prod(I+B^{2^k}); all matmuls fp32."""
import numpy as np
from contextlib import ExitStack
import concourse.bass as bass
import concourse.mybir as mybir
from concourse.bass_types import AP

F32 = mybir.dt.float32
AF = mybir.ActivationFunctionType
ALU = mybir.AluOpType
AX = mybir.AxisListType

TILE = 256
G = 2
H = 8
FC = 8
D = 1024
NE = 64
NLEV = 7


def host_constants():
    i_idx = np.arange(128)[:, None]
    j_idx = np.arange(128)[None, :]
    return {
        "ident": np.eye(128, dtype=np.float32),
        "m_low_neg": np.where(i_idx > j_idx, -1.0, 0.0).astype(np.float32),
        "m_up_neg": np.where(i_idx < j_idx, -1.0, 0.0).astype(np.float32),
        "m_upincl": np.where(i_idx <= j_idx, 1.0, 0.0).astype(np.float32),
        "ones_col": np.ones((128, 1), np.float32),
        "ones_row": np.ones((1, 128), np.float32),
    }


def host_inputs(full, core):
    x = np.asarray(full["L0_attn_hidden_states"])
    xs = np.ascontiguousarray(x[2 * core:2 * core + 2].reshape(4096, D))
    conv = np.stack([np.asarray(full["L0_attn_conv_q_w"]),
                     np.asarray(full["L0_attn_conv_k_w"]),
                     np.asarray(full["L0_attn_conv_v_w"])])
    conv = conv.reshape(3, FC, 128, 4).transpose(2, 0, 1, 3).reshape(128, 96)
    bmix = np.concatenate([np.asarray(full["L0_attn_b_w"]),
                           np.asarray(full["L0_attn_mix_w"])], axis=1)
    m = {
        "x_in": xs,
        "wq": np.asarray(full["L0_attn_q_w"]),
        "wk": np.asarray(full["L0_attn_k_w"]),
        "wv": np.asarray(full["L0_attn_v_w"]),
        "wo": np.asarray(full["L0_attn_o_w"]),
        "convw": np.ascontiguousarray(conv),
        "bmix_w": np.ascontiguousarray(bmix),
        "mix_b": np.asarray(full["L0_attn_mix_b"]).reshape(1, 8),
        "mix_bias": np.asarray(full["L0_attn_mix_bias"]).reshape(1, 8),
        "onorm": np.asarray(full["L0_attn_o_norm_w"]).reshape(128, 1),
        "sel_r": np.ascontiguousarray(np.asarray(full["L0_ffn_expert_sel"]).T),
        "keys": np.asarray(full["L0_ffn_keys"]),
        "values": np.asarray(full["L0_ffn_values"]),
    }
    m.update(host_constants())
    return {k: np.ascontiguousarray(v, dtype=np.float32) for k, v in m.items()}


def build(NT=16, tiles_per_seq=8):
    nc = bass.Bass("TRN2", target_bir_lowering=False)
    B = Builder()
    ctx = ExitStack()

    dram_in = {}
    def din(name, shape):
        dram_in[name] = nc.dram_tensor(name, shape, F32, kind="ExternalInput").ap()
        return dram_in[name]

    x_in = din("x_in", [4096, D])
    wq = din("wq", [D, D]); wk = din("wk", [D, D]); wv = din("wv", [D, D])
    wo = din("wo", [D, D])
    convw_d = din("convw", [128, 96])
    bmix_d = din("bmix_w", [D, 16])
    mixb_d = din("mix_b", [1, 8]); mixbias_d = din("mix_bias", [1, 8])
    onorm_d = din("onorm", [128, 1])
    sel_d = din("sel_r", [D, NE])
    keys_d = din("keys", [NE, D, 128])
    values_d = din("values", [NE, 128, D])
    for cn in ("ident", "m_low_neg", "m_up_neg", "m_upincl"):
        din(cn, [128, 128])
    din("ones_col", [128, 1]); din("ones_row", [1, 128])

    out_d = nc.dram_tensor("out", [4096, D], F32, kind="ExternalOutput").ap()
    gdram = nc.dram_tensor("g_dram", [NE, TILE], F32, kind="Internal").ap()

    sbufs = {}
    def sb(name, shape):
        h = ctx.enter_context(nc.sbuf_tensor(name, shape, F32))
        sbufs[name] = h
        return h

    wbuf = [sb(f"wbuf{i}", [128, FC * 512]) for i in range(2)]
    wo_sb = [sb(f"wo{h}", [128, D]) for h in range(H)]
    sel_sb = sb("sel_sb", [128, FC * NE])
    ke_sb = [sb(f"ke{i}", [128, D]) for i in range(2)]
    ve_sb = [sb(f"ve{i}", [128, D]) for i in range(2)]
    x_tok = [sb(f"x_tok{g}", [128, D]) for g in range(G)]
    lnt = [sb(f"lnt{g}", [128, D]) for g in range(G)]
    ln1_fm = [sb(f"lnf{c}", [128, TILE]) for c in range(FC)]
    ln2_fm = ln1_fm
    pre = {p: [sb(f"{p}_pre{c}", [128, TILE + 3]) for c in range(FC)] for p in "qkv"}
    fm = {p: [pre[p][c][:, 3:3 + TILE] for c in range(FC)] for p in "qkv"}
    tails = {p: sb(f"{p}_tail", [128, 3 * FC]) for p in "qkv"}
    cv = {p: sb(f"cv_{p}", [128, TILE]) for p in "qkv"}
    convw = sb("convw_sb", [128, 96])
    bmix_sb = sb("bmix_sb", [128, FC * 16])
    biasrow = sb("biasrow", [1, 16])
    tmp8 = sb("tmp8", [1, 8])
    onorm_sb = sb("onorm_sb", [128, 1])
    bm = [sb(f"bm{g}", [128, 16]) for g in range(G)]
    sqb = [sb(f"sqb{g}", [128, 8]) for g in range(G)]
    S_sb = [sb(f"S{h}", [128, 128]) for h in range(H)]
    consts = {cn: sb(cn + "_sb", list(np.shape(host_constants()[cn])))
              for cn in ("ident", "m_low_neg", "m_up_neg", "m_upincl",
                          "ones_col", "ones_row")}
    inst = []
    for pn in range(4):
        dd = {}
        names = [("k_tok", [128, 128]), ("uw0", [128, 256]),
                 ("ktf", [128, 128]),
                 ("attnT", [128, 128]), ("u_p", [128, 128]),
                 ("Ma", [128, 128]), ("sq_i", [128, 128]),
                 ("kss", [128, 1]), ("kss2", [128, 1]), ("rsqk", [128, 1]),
                 ("nwa", [128, 1]), ("nwb", [128, 1])]
        for nm, shp in names:
            dd[nm] = sb(f"{nm}_{pn}", shp)
        dd["P"] = [sb(f"P{k}_{pn}", [128, 128]) for k in range(2)]
        dd["PT"] = [sb(f"PT{k}_{pn}", [128, 128]) for k in range(2)]
        dd["_p"] = pn
        inst.append(dd)
    nw1 = sb("nw1", [128, 1]); nw2 = sb("nw2", [128, 1])
    po = []
    for j in range(3):
        po.append({nm: sb(f"po_{nm}{j}", [128, 1]) for nm in ("oss", "oss2", "rms", "nwa", "nwb")})
        po[j]["sq"] = sb(f"po_sq{j}", [128, 128])
        po[j]["ofm"] = sb(f"po_ofm{j}", [128, 128])
    qhf_t = [sb(f"qhf_t{h}", [128, TILE]) for h in range(H)]
    v_tok_t = [sb(f"v_tok_t{h}", [128, 128]) for h in range(H)]
    o_raw = [sb(f"o_raw{h}", [128, 128]) for h in range(H)]
    qsq_t = [sb(f"qsq_t{h}", [128, TILE]) for h in range(2)]
    qrow_t = [sb(f"qrow_t{h}", [1, TILE]) for h in range(2)]
    rsqq_t = [sb(f"rsqq_t{h}", [1, TILE]) for h in range(2)]
    nra_t = [sb(f"nra_t{h}", [1, TILE]) for h in range(2)]
    ml = [sb(f"ml{g}", [128, 1]) for g in range(G)]
    rl = [sb(f"rl{g}", [128, 1]) for g in range(G)]
    ssl = sb("ssl", [128, 1]); v1l = sb("v1l", [128, 1]); m2l = sb("m2l", [128, 1])
    sgate = [sb(f"sgate{g}", [128, NE]) for g in range(G)]
    top8 = sb("top8", [128, 8]); gmask = sb("gmask", [128, NE])
    g_fm = sb("g_fm", [64, TILE])
    sc_sb = [sb(f"sc{i}", [128, TILE]) for i in range(2)]
    gb_sb = [sb(f"gb{i}", [128, 2 * TILE]) for i in range(2)]
    g8 = [sb(f"g8_{i}", [1, 2 * TILE]) for i in range(2)]

    psum = [ctx.enter_context(nc.psum_tensor(f"pb{i}", [128, 512], F32)) for i in range(8)]
    PB = [p.ap() for p in psum]

    AOB = [4, 5]
    OUT2B = {(0, 0): 4, (0, 1): 5, (1, 0): 0, (1, 1): 1}
    _rot = {"tp": 0, "mm": 0, "small": 0}
    def rot(kind):
        lst = {"tp": [0, 1], "mm": [2, 3], "small": [6, 7]}[kind]
        i = _rot[kind] % len(lst); _rot[kind] += 1
        return lst[i]

    def dma(out_ap, in_ap, stream, reads=(), writes=(), tag=""):
        return B.dma(out_ap, in_ap, stream, reads=reads, writes=writes, tag=tag)

    def mm(bank, out_ap, lhsT, rhs, reads, start=True, stop=True, tag=""):
        def e(eng, out_ap=out_ap, lhsT=lhsT, rhs=rhs, start=start, stop=stop):
            return eng.matmul(out_ap, lhsT, rhs, start=start, stop=stop)
        return B.op("tensor", e, reads=list(reads), writes=[f"pb{bank}"], tag=tag)

    def tr(bank, out_ap, in_ap, reads, tag=""):
        def e(eng, out_ap=out_ap, in_ap=in_ap):
            return eng.transpose(out_ap, in_ap, consts["ident"][:, :])
        return B.op("tensor", e, reads=list(reads) + ["ident_sb"], writes=[f"pb{bank}"], tag=tag)

    def vop(emit, reads, writes, tag=""):
        return B.op("vector", emit, reads=reads, writes=writes, tag=tag)
    def sop(emit, reads, writes, tag=""):
        return B.op("scalar", emit, reads=reads, writes=writes, tag=tag)
    def gop(emit, reads, writes, tag=""):
        return B.op("gpsimd", emit, reads=reads, writes=writes, tag=tag)

    def emit_rsqrt(v_ap, vname, out_ap, outname, t1_ap, t1n, t2_ap, t2n, tag):
        vop(lambda e: e.reciprocal(t1_ap, v_ap), reads=[vname], writes=[t1n], tag=tag + "_rc")
        sop(lambda e: e.activation(t2_ap, t1_ap, AF.Sqrt), reads=[t1n], writes=[t2n], tag=tag + "_sq")
        gop(lambda e: e.tensor_mul(t1_ap, t2_ap, t2_ap), reads=[t2n], writes=[t1n], tag=tag + "_n1")
        gop(lambda e: e.tensor_mul(t1_ap, t1_ap, v_ap), reads=[t1n, vname], writes=[t1n], tag=tag + "_n2")
        gop(lambda e: e.tensor_scalar(t1_ap, t1_ap, -0.5, 1.5, ALU.mult, ALU.add), reads=[t1n], writes=[t1n], tag=tag + "_n3")
        gop(lambda e: e.tensor_mul(out_ap, t2_ap, t1_ap), reads=[t1n, t2n], writes=[outname], tag=tag + "_n4")

    def ln_token_major(g, t):
        xn, lname = f"x_tok{g}", f"lnt{g}"
        mname, rname = f"m{g}", f"r{g}"
        x_ap = x_tok[g][:, :]; ln_ap = lnt[g][:, :]
        m_t, r_t = ml[g], rl[g]
        vop(lambda e: e.reduce_sum(m_t[:, :], x_ap, axis=AX.X), reads=[xn], writes=[mname], tag=f"lnsum{t}{g}")
        vop(lambda e: e.tensor_scalar(m_t[:, :], m_t[:, :], 1.0 / 1024.0, None, ALU.mult),
            reads=[mname], writes=[mname], tag=f"lnm{t}{g}")
        sop(lambda e: e.activation(ln_ap, x_ap, AF.Square, accum_out=ssl[:, :]),
            reads=[xn], writes=["ssl", lname], tag=f"lnsq{t}{g}")
        vop(lambda e: e.tensor_scalar(v1l[:, :], ssl[:, :], 1.0 / 1024.0, 1e-5, ALU.mult, ALU.add),
            reads=["ssl"], writes=["v1l"], tag=f"lnv1{t}{g}")
        vop(lambda e: e.tensor_mul(m2l[:, :], m_t[:, :], m_t[:, :]), reads=[mname], writes=["m2l"], tag=f"lnm2{t}{g}")
        vop(lambda e: e.tensor_sub(v1l[:, :], v1l[:, :], m2l[:, :]), reads=["v1l", "m2l"], writes=["v1l"], tag=f"lnv{t}{g}")
        emit_rsqrt(v1l[:, :], "v1l", r_t[:, :], rname, nw1[:, :], "nw1", nw2[:, :], "nw2", f"lnr{t}{g}")
        vop(lambda e: e.tensor_scalar(ln_ap, x_ap, m_t[:, :], r_t[:, :], ALU.subtract, ALU.mult),
            reads=[xn, mname, rname], writes=[lname], tag=f"lnn{t}{g}")

    def attn_instance(t, g, h, tail_hook=None):
        d = inst[h % 4]
        pn = d["_p"]
        def N(nm):
            return f"{nm}_{pn}"
        mmb = {0: [0, 2], 1: [1, 3], 2: [6, 7], 3: [4, 5]}[pn]
        mi = [0]
        def mrot():
            b = mmb[mi[0] % 2]; mi[0] += 1
            return b
        qs = fm["q"][h][:, g * 128:(g + 1) * 128]
        ks = fm["k"][h][:, g * 128:(g + 1) * 128]
        vs = fm["v"][h][:, g * 128:(g + 1) * 128]
        kn, qn, vn = f"k_pre{h}", f"q_pre{h}", f"v_pre{h}"
        k_tok = d["k_tok"]; uw0 = d["uw0"]; uw_s = d["uw0"]
        ktf = d["ktf"]; wtf = d["ktf"]; attnT = d["attnT"]; u_p = d["u_p"]
        qhf = qhf_t[h][:, g * 128:(g + 1) * 128]
        Ma = d["Ma"]
        P = d["P"]; PT = d["PT"]; sq_i = d["sq_i"]
        sqb_col = sqb[g][:, h:h + 1]
        gcol = bm[g][:, 8 + h:9 + h]
        Sn = f"S{h}"

        b_ = mrot()
        tr(b_, PB[b_][:, 0:128], ks, reads=[kn], tag=f"kT{t}{g}{h}")
        sop(lambda e, b_=b_: e.copy(k_tok[:, :], PB[b_][:, 0:128]), reads=[f"pb{b_}"], writes=[N("k_tok")], tag="kTe")
        sop(lambda e: e.activation(sq_i[:, :], k_tok[:, :], AF.Square, accum_out=d["kss"][:, :]),
            reads=[N("k_tok")], writes=[N("kss"), N("sq_i")], tag="ksq")
        vop(lambda e: e.tensor_scalar(d["kss2"][:, :], d["kss"][:, :], 1e-12, None, ALU.add),
            reads=[N("kss")], writes=[N("kss2")], tag="keps")
        emit_rsqrt(d["kss2"][:, :], N("kss2"), d["rsqk"][:, :], N("rsqk"),
                   d["nwa"][:, :], N("nwa"), d["nwb"][:, :], N("nwb"), f"krsq{pn}")
        yield
        b_ = mrot()
        tr(b_, PB[b_][:, 0:128], vs, reads=[vn], tag=f"vT{t}{g}{h}")
        sop(lambda e, b_=b_: e.copy(v_tok_t[h][:, :], PB[b_][:, 0:128]), reads=[f"pb{b_}"], writes=[f"v_tok_t{h}"], tag="vTe")
        sop(lambda e: e.activation(k_tok[:, :], k_tok[:, :], AF.Identity, scale=d["rsqk"][:, :]),
            reads=[N("k_tok"), N("rsqk")], writes=[N("k_tok")], tag="khat")
        sop(lambda e: e.activation(uw0[:, 128:256], k_tok[:, :], AF.Identity, scale=sqb_col),
            reads=[N("k_tok"), f"sqb{g}"], writes=[N("uw0")], tag="ktil")
        sop(lambda e: e.activation(uw0[:, 0:128], v_tok_t[h][:, :], AF.Identity, scale=sqb_col),
            reads=[f"v_tok_t{h}", f"sqb{g}"], writes=[N("uw0")], tag="vtil")
        yield
        b_ = mrot()
        tr(b_, PB[b_][:, 0:128], uw0[:, 128:256], reads=[N("uw0")], tag="ktfT")
        sop(lambda e, b_=b_: e.copy(ktf[:, :], PB[b_][:, 0:128]), reads=[f"pb{b_}"], writes=[N("ktf")], tag="ktfE")
        yield
        b0 = mrot()
        mm(b0, PB[b0][:, 0:128], ktf[:, :], ktf[:, :], reads=[N("ktf")], tag="Gt")
        def e(eng, b0=b0):
            return eng.tensor_tensor(P[0][:, :], PB[b0][:, 0:128], consts["m_low_neg"][:, :], ALU.mult)
        vop(e, reads=[f"pb{b0}", "m_low_neg_sb"], writes=[N("P0")], tag="Bev")
        def e(eng, b0=b0):
            return eng.tensor_tensor(PT[0][:, :], PB[b0][:, 0:128], consts["m_up_neg"][:, :], ALU.mult)
        vop(e, reads=[f"pb{b0}", "m_up_neg_sb"], writes=[N("PT0")], tag="BTev")
        vop(lambda e: e.tensor_add(Ma[:, :], PT[0][:, :], consts["ident"][:, :]),
            reads=[N("PT0"), "ident_sb"], writes=[N("Ma")], tag="M1")
        yield
        Man = N("Ma")
        for k in range(1, NLEV):
            sl_ = k % 2; slp = (k - 1) % 2
            bk = mrot()
            mm(bk, PB[bk][:, 0:128], PT[slp][:, :], P[slp][:, :],
               reads=[N(f"PT{slp}"), N(f"P{slp}")], tag=f"sqP{k}")
            ep = sop if k % 2 == 0 else vop
            def e(eng, sl_=sl_, bk=bk):
                if hasattr(eng, "copy"):
                    return eng.copy(P[sl_][:, :], PB[bk][:, 0:128])
                return eng.tensor_copy(P[sl_][:, :], PB[bk][:, 0:128])
            ep(e, reads=[f"pb{bk}"], writes=[N(f"P{sl_}")], tag=f"sqPe{k}")
            if k < NLEV - 1:
                bk2 = mrot()
                mm(bk2, PB[bk2][:, 0:128], P[slp][:, :], PT[slp][:, :],
                   reads=[N(f"P{slp}"), N(f"PT{slp}")], tag=f"sqT{k}")
                ep2 = vop if k % 2 == 0 else sop
                def e(eng, sl_=sl_, bk2=bk2):
                    if hasattr(eng, "copy"):
                        return eng.copy(PT[sl_][:, :], PB[bk2][:, 0:128])
                    return eng.tensor_copy(PT[sl_][:, :], PB[bk2][:, 0:128])
                ep2(e, reads=[f"pb{bk2}"], writes=[N(f"PT{sl_}")], tag=f"sqTe{k}")
            bk3 = mrot()
            mm(bk3, PB[bk3][:, 0:128], P[sl_][:, :], Ma[:, :],
               reads=[N(f"P{sl_}"), Man], tag=f"ap{k}")
            def e(eng, bk3=bk3):
                return eng.tensor_add(Ma[:, :], Ma[:, :], PB[bk3][:, 0:128])
            vop(e, reads=[f"pb{bk3}", Man], writes=[Man], tag=f"apE{k}")
            yield
        bk = mrot()
        mm(bk, PB[bk][:, 0:256], Ma[:, :], uw0[:, :], reads=[Man, N("uw0")], tag="uwmm")
        def e(eng, bk=bk):
            return eng.activation(uw_s[:, :], PB[bk][:, 0:256], AF.Identity, scale=sqb_col)
        sop(e, reads=[f"pb{bk}", f"sqb{g}"], writes=[N("uw0")], tag="uwE")
        b_ = mrot()
        tr(b_, PB[b_][:, 0:128], uw_s[:, 128:256], reads=[N("uw0")], tag="wT")
        sop(lambda e, b_=b_: e.copy(wtf[:, :], PB[b_][:, 0:128]), reads=[f"pb{b_}"], writes=[N("ktf")], tag="wTe")
        yield
        bk = mrot()
        mm(bk, PB[bk][:, 0:128], wtf[:, :], S_sb[h][:, :], reads=[N("ktf"), Sn], tag="wS")
        def e(eng, bk=bk):
            return eng.tensor_sub(u_p[:, :], uw_s[:, 0:128], PB[bk][:, 0:128])
        vop(e, reads=[f"pb{bk}", N("uw0")], writes=[N("u_p")], tag="uprime")
        bk = mrot()
        mm(bk, PB[bk][:, 0:128], ks, qhf, reads=[kn, f"qhf_t{h}"], tag="aT")
        def e(eng, bk=bk):
            return eng.scalar_tensor_tensor(attnT[:, :], PB[bk][:, 0:128], d["rsqk"][:, :],
                                            consts["m_upincl"][:, :], ALU.mult, ALU.mult)
        vop(e, reads=[f"pb{bk}", N("rsqk"), "m_upincl_sb"], writes=[N("attnT")], tag="aTe")
        yield
        bk = mrot()
        mm(bk, PB[bk][:, 0:128], qhf, S_sb[h][:, :], reads=[f"qhf_t{h}", Sn], start=True, stop=False, tag="qS")
        mm(bk, PB[bk][:, 0:128], attnT[:, :], u_p[:, :], reads=[N("attnT"), N("u_p")], start=False, stop=True, tag="au")
        def e(eng, bk=bk):
            return eng.tensor_sub(o_raw[h][:, :], PB[bk][:, 0:128], v_tok_t[h][:, :])
        vop(e, reads=[f"pb{bk}", f"v_tok_t{h}"], writes=[f"o_raw{h}"], tag="gsub")
        bk = mrot()
        mm(bk, PB[bk][:, 0:128], k_tok[:, :], u_p[:, :], reads=[N("k_tok"), N("u_p")], tag="Sup")
        def e(eng, bk=bk):
            return eng.tensor_add(S_sb[h][:, :], S_sb[h][:, :], PB[bk][:, 0:128])
        vop(e, reads=[f"pb{bk}", Sn], writes=[Sn], tag="SupE")
        if tail_hook is not None:
            tail_hook(h)

    dma(convw[:, :], convw_d, "convw", writes=["convw_sb"])
    dma(biasrow[0:1, 8:16], mixb_d, "biasrow", writes=["biasrow"])
    dma(tmp8[:, :], mixbias_d, "tmp8", writes=["tmp8"])
    dma(onorm_sb[:, :], onorm_d, "onorm", writes=["onorm_sb"])
    for cn in consts:
        nm = cn + "_sb"
        dma(sbufs[nm][:, :], dram_in[cn], nm, writes=[nm])
    sel_src = AP(tensor=sel_d.tensor, offset=0, ap=[[NE, 128], [128 * NE, FC], [1, NE]])
    dma(sel_sb[:, :], sel_src, "sel_sb", writes=["sel_sb"])
    for h in range(H):
        dma(wo_sb[h][:, :], wo[h * 128:(h + 1) * 128, :], f"wo{h}", writes=[f"wo{h}"])
    gop(lambda e: e.memset(biasrow[0:1, 0:8], 0.0), reads=["biasrow"], writes=["biasrow"], tag="biasz")
    vop(lambda e: e.tensor_add(biasrow[0:1, 8:16], biasrow[0:1, 8:16], tmp8[0:1, :]),
        reads=["biasrow", "tmp8"], writes=["biasrow"], tag="biasadd")
    for h in range(H):
        def e(eng, h=h):
            return eng.tensor_scalar(wo_sb[h][:, :], wo_sb[h][:, :], onorm_sb[:, :], None, ALU.mult)
        vop(e, reads=[f"wo{h}", "onorm_sb"], writes=[f"wo{h}"], tag="wo_scale")
    bmix_src = AP(tensor=bmix_d.tensor, offset=0, ap=[[16, 128], [128 * 16, FC], [1, 16]])
    dma(bmix_sb[:, :], bmix_src, "bmix_sb", writes=["bmix_sb"])

    for t in range(NT):
        seq_start = (t % tiles_per_seq) == 0
        for g in range(G):
            r0 = t * TILE + g * 128
            dma(x_tok[g][:, :], x_in[r0:r0 + 128, :], f"x_tok{g}",
                writes=[f"x_tok{g}"], tag=f"xin{t}g{g}")
        for g in range(G):
            ln_token_major(g, t)
            for c in range(FC):
                bk = rot("tp")
                tr(bk, PB[bk][:, 0:128], lnt[g][:, c * 128:(c + 1) * 128],
                   reads=[f"lnt{g}"], tag=f"ln1T{t}{g}{c}")
                def e(eng, c=c, g=g, bk=bk):
                    return eng.copy(ln1_fm[c][:, g * 128:(g + 1) * 128], PB[bk][:, 0:128])
                sop(e, reads=[f"pb{bk}"], writes=[f"lnf{c}"], tag=f"ln1E{t}{g}{c}")
        hoist_queue = []
        def _hoist_unit_list():
            units = []
            for h_ in range(H):
                units.append(lambda h_=h_: (_emit_qss(h_), (_emit_qsquare(h_ + 2) if h_ + 2 < H else None)))
                if h_ >= 1:
                    units.append(lambda h_=h_: _emit_qbc(h_ - 1))
            units.append(lambda: _emit_qbc(H - 1))
            return units
        def _emit_qbc(h):
            hp2 = h % 2
            qfm_h = fm["q"][h]
            qnh = f"q_pre{h}"
            bk2 = rot("small")
            mm(bk2, PB[bk2][:, 0:TILE], consts["ones_row"][:, :], rsqq_t[hp2][0:1, :],
               reads=["ones_row_sb", f"rsqq_t{hp2}"], tag=f"qbcH{h}")
            def e(eng, bk2=bk2, qfm_h=qfm_h, h=h):
                return eng.tensor_tensor(qhf_t[h][:, :], qfm_h, PB[bk2][:, 0:TILE], ALU.mult)
            vop(e, reads=[qnh, f"pb{bk2}"], writes=[f"qhf_t{h}"], tag=f"qhatH{h}")
        def _emit_qsquare(h):
            hp2 = h % 2
            qfm_h = fm["q"][h]
            qnh = f"q_pre{h}"
            sop(lambda e, qfm_h=qfm_h, hp2=hp2: e.activation(qsq_t[hp2][:, :], qfm_h, AF.Square),
                reads=[qnh], writes=[f"qsq_t{hp2}"], tag=f"qsqH{h}")
        def _emit_qss(h):
            hp2 = h % 2
            qfm_h = fm["q"][h]
            qnh = f"q_pre{h}"
            bk = rot("small")
            mm(bk, PB[bk][0:1, 0:TILE], consts["ones_col"][:, :], qsq_t[hp2][:, :],
               reads=["ones_col_sb", f"qsq_t{hp2}"], tag=f"qssH{h}")
            def e(eng, bk=bk, hp2=hp2):
                return eng.tensor_scalar(qrow_t[hp2][0:1, :], PB[bk][0:1, 0:TILE], 1e-12, None, ALU.add)
            vop(e, reads=[f"pb{bk}"], writes=[f"qrow_t{hp2}"], tag=f"qepsH{h}")
            emit_rsqrt(qrow_t[hp2][0:1, :], f"qrow_t{hp2}", rsqq_t[hp2][0:1, :], f"rsqq_t{hp2}",
                       nra_t[hp2][0:1, :], f"nra_t{hp2}", rsqq_t[hp2][0:1, :], f"rsqq_t{hp2}", f"qrsqH{h}")
        for pi, (p, wd) in enumerate((("q", wq), ("k", wk), ("v", wv))):
            for c in range(FC):
                if seq_start:
                    gop(lambda e, p=p, c=c: e.memset(pre[p][c][:, 0:3], 0.0),
                        reads=[], writes=[f"{p}_pre{c}"], tag=f"tz{p}{c}")
                else:
                    gop(lambda e, p=p, c=c: e.tensor_copy(pre[p][c][:, 0:3], tails[p][:, c * 3:c * 3 + 3]),
                        reads=[f"{p}_tail"], writes=[f"{p}_pre{c}"], tag=f"tc{p}{c}")
            for hf in range(2):
                wslot = (pi * 2 + hf) % 2
                wsrc = AP(tensor=wd.tensor, offset=hf * 512,
                          ap=[[D, 128], [128 * D, FC], [1, 512]])
                dma(wbuf[wslot][:, :], wsrc, f"wbuf{wslot}", writes=[f"wbuf{wslot}"], tag=f"w{p}{t}h{hf}")
                for oc4 in range(4):
                    oc = hf * 4 + oc4
                    bk = rot("mm")
                    for ic in range(FC):
                        mm(bk, PB[bk][:, 0:TILE],
                           wbuf[wslot][:, ic * 512 + oc4 * 128: ic * 512 + oc4 * 128 + 128],
                           ln1_fm[ic][:, :],
                           reads=[f"wbuf{wslot}", f"lnf{ic}"],
                           start=(ic == 0), stop=(ic == FC - 1), tag=f"prj{p}{oc}i{ic}")
                    def e(eng, p=p, oc=oc, bk=bk):
                        return eng.copy(pre[p][oc][:, 3:3 + TILE], PB[bk][:, 0:TILE])
                    sop(e, reads=[f"pb{bk}"], writes=[f"{p}_pre{oc}"], tag=f"prjE{p}{oc}")
                    gop(lambda e, p=p, oc=oc: e.tensor_copy(tails[p][:, oc * 3:oc * 3 + 3], pre[p][oc][:, TILE:TILE + 3]),
                        reads=[f"{p}_pre{oc}"], writes=[f"{p}_tail"], tag=f"ts{p}{oc}")
                    pcol = {"q": 0, "k": 1, "v": 2}[p]
                    def e0(eng, p=p, oc=oc, pcol=pcol):
                        return eng.tensor_scalar(cv[p][:, :], pre[p][oc][:, 0:TILE],
                                                 convw[:, pcol * 32 + oc * 4: pcol * 32 + oc * 4 + 1], None, ALU.mult)
                    vop(e0, reads=[f"{p}_pre{oc}", "convw_sb"], writes=[f"cv_{p}"], tag=f"cnv{p}{oc}0")
                    for i in (1, 2, 3):
                        def ei(eng, p=p, oc=oc, i=i, pcol=pcol):
                            return eng.scalar_tensor_tensor(cv[p][:, :], pre[p][oc][:, i:i + TILE],
                                                            convw[:, pcol * 32 + oc * 4 + i: pcol * 32 + oc * 4 + i + 1],
                                                            cv[p][:, :], ALU.mult, ALU.add)
                        vop(ei, reads=[f"{p}_pre{oc}", "convw_sb", f"cv_{p}"], writes=[f"cv_{p}"], tag=f"cnv{p}{oc}{i}")
                    def esilu(eng, p=p, oc=oc):
                        return eng.activation(fm[p][oc], cv[p][:, :], AF.Silu)
                    sop(esilu, reads=[f"cv_{p}"], writes=[f"{p}_pre{oc}"], tag=f"silu{p}{oc}")
                    if p == "q":
                        if oc == 6:
                            _emit_qsquare(0)
                        if oc == FC - 1:
                            _emit_qsquare(1)
                            hoist_queue.extend(_hoist_unit_list())
                    elif hoist_queue:
                        hoist_queue.pop(0)()
        for g in range(G):
            bk = rot("small")
            for ic in range(FC):
                mm(bk, PB[bk][:, 0:16], ln1_fm[ic][:, g * 128:(g + 1) * 128],
                   bmix_sb[:, ic * 16:(ic + 1) * 16],
                   reads=[f"lnf{ic}", "bmix_sb"], start=(ic == 0), stop=False, tag=f"bm{g}i{ic}")
            mm(bk, PB[bk][:, 0:16], consts["ones_row"][:, :], biasrow[0:1, :],
               reads=["ones_row_sb", "biasrow"], start=False, stop=True, tag=f"bmb{g}")
            def e(eng, g=g, bk=bk):
                return eng.activation(bm[g][:, :], PB[bk][:, 0:16], AF.Sigmoid)
            sop(e, reads=[f"pb{bk}"], writes=[f"bm{g}"], tag=f"bmS{g}")
            def e2(eng, g=g):
                return eng.activation(sqb[g][:, :], bm[g][:, 0:8], AF.Sqrt)
            sop(e2, reads=[f"bm{g}"], writes=[f"sqb{g}"], tag=f"sqb{g}")
        for g in range(G):
            if seq_start and g == 0:
                for h in range(H):
                    vop(lambda e, h=h: e.memset(S_sb[h][:, :], 0.0),
                        reads=[], writes=[f"S{h}"], tag=f"Sz{h}")
            def post_chain(h):
                d3 = po[h % 3]
                p3 = h % 3
                gcol2 = bm[g][:, 8 + h:9 + h]
                def e(eng, h=h, gcol2=gcol2):
                    return eng.scalar_tensor_tensor(o_raw[h][:, :], o_raw[h][:, :], gcol2,
                                                    v_tok_t[h][:, :], ALU.mult, ALU.add)
                vop(e, reads=[f"o_raw{h}", f"bm{g}", f"v_tok_t{h}"], writes=[f"o_raw{h}"], tag=f"gmix{h}")
                sop(lambda e, h=h, d3=d3: e.activation(d3["sq"][:, :], o_raw[h][:, :], AF.Square, accum_out=d3["oss"][:, :]),
                    reads=[f"o_raw{h}"], writes=[f"po_oss{p3}", f"po_sq{p3}"], tag=f"osq{h}")
                vop(lambda e, d3=d3: e.tensor_scalar(d3["oss2"][:, :], d3["oss"][:, :], 1.0 / 128.0, 1e-5, ALU.mult, ALU.add),
                    reads=[f"po_oss{p3}"], writes=[f"po_oss2{p3}"], tag=f"oeps{h}")
                emit_rsqrt(d3["oss2"][:, :], f"po_oss2{p3}", d3["rms"][:, :], f"po_rms{p3}",
                           d3["nwa"][:, :], f"po_nwa{p3}", d3["nwb"][:, :], f"po_nwb{p3}", f"orsq{h}")
                sop(lambda e, h=h, d3=d3: e.activation(o_raw[h][:, :], o_raw[h][:, :], AF.Identity, scale=d3["rms"][:, :]),
                    reads=[f"o_raw{h}", f"po_rms{p3}"], writes=[f"o_raw{h}"], tag=f"onrm{h}")
            def post_pe(h):
                d3 = po[h % 3]
                p3 = h % 3
                bko = [0, 2, 1, 3][h % 4]
                tr(bko, PB[bko][:, 0:128], o_raw[h][:, :], reads=[f"o_raw{h}"], tag=f"oT{h}")
                sop(lambda e, bko=bko, d3=d3: e.copy(d3["ofm"][:, :], PB[bko][:, 0:128]),
                    reads=[f"pb{bko}"], writes=[f"po_ofm{p3}"], tag=f"oTe{h}")
                for half in range(2):
                    ob = AOB[half]
                    mm(ob, PB[ob][:, 0:512], d3["ofm"][:, :], wo_sb[h][:, half * 512:(half + 1) * 512],
                       reads=[f"po_ofm{p3}", f"wo{h}"], start=(h == 0), stop=(h == H - 1), tag=f"ao{h}_{half}")
            for hp in range(0, H, 4):
                hs = list(range(hp, min(hp + 4, H)))
                gens = [attn_instance(t, g, h_) for h_ in hs]
                alive = [True] * len(gens)
                while any(alive):
                    for i_ in range(len(gens)):
                        if alive[i_]:
                            try:
                                next(gens[i_])
                            except StopIteration:
                                alive[i_] = False
            post_chain(0); post_chain(1); post_chain(2)
            for h in range(H):
                if h + 3 < H:
                    post_chain(h + 3)
                post_pe(h)
            for half in range(2):
                ob = AOB[half]
                def e(eng, g=g, half=half, ob=ob):
                    return eng.tensor_add(x_tok[g][:, half * 512:(half + 1) * 512],
                                          x_tok[g][:, half * 512:(half + 1) * 512],
                                          PB[ob][:, 0:512])
                vop(e, reads=[f"pb{ob}", f"x_tok{g}"], writes=[f"x_tok{g}"], tag=f"res1{g}{half}")
        for g in range(G):
            ln_token_major(g, t)
            for c in range(FC):
                bk = rot("tp")
                tr(bk, PB[bk][:, 0:128], lnt[g][:, c * 128:(c + 1) * 128],
                   reads=[f"lnt{g}"], tag=f"ln2T{t}{g}{c}")
                def e(eng, c=c, g=g, bk=bk):
                    return eng.copy(ln2_fm[c][:, g * 128:(g + 1) * 128], PB[bk][:, 0:128])
                sop(e, reads=[f"pb{bk}"], writes=[f"lnf{c}"], tag=f"ln2E{t}{g}{c}")
        for e_i in (0, 1):
            sl = e_i % 2
            ksrc = AP(tensor=keys_d.tensor, offset=e_i * D * 128,
                      ap=[[128, 128], [128 * 128, FC], [1, 128]])
            dma(ke_sb[sl][:, :], ksrc, f"ke{sl}", writes=[f"ke{sl}"], tag=f"keL{e_i}pre")
            dma(ve_sb[sl][:, :], values_d[e_i, :, :], f"ve{sl}", writes=[f"ve{sl}"], tag=f"veL{e_i}pre")
        for g in range(G):
            bk = rot("small")
            for ic in range(FC):
                mm(bk, PB[bk][:, 0:NE], ln2_fm[ic][:, g * 128:(g + 1) * 128],
                   sel_sb[:, ic * NE:(ic + 1) * NE],
                   reads=[f"lnf{ic}", "sel_sb"], start=(ic == 0), stop=(ic == FC - 1), tag=f"lg{g}{ic}")
            def e(eng, g=g, bk=bk):
                return eng.activation(sgate[g][:, :], PB[bk][:, 0:NE], AF.Sigmoid)
            sop(e, reads=[f"pb{bk}"], writes=[f"sgate{g}"], tag=f"sgS{g}")
            vop(lambda e, g=g: e.max(out=top8[:, :], in_=sgate[g][:, :]),
                reads=[f"sgate{g}"], writes=["top8"], tag=f"top8{g}")
            def e2(eng, g=g):
                return eng.tensor_scalar(gmask[:, :], sgate[g][:, :], top8[:, 3:4], None, ALU.is_ge)
            vop(e2, reads=[f"sgate{g}", "top8"], writes=["gmask"], tag=f"gmask{g}")
            def e3(eng, g=g):
                return eng.tensor_mul(sgate[g][:, :], sgate[g][:, :], gmask[:, :])
            vop(e3, reads=[f"sgate{g}", "gmask"], writes=[f"sgate{g}"], tag=f"gmul{g}")
            bk2 = rot("tp")
            tr(bk2, PB[bk2][0:64, 0:128], sgate[g][:, 0:NE], reads=[f"sgate{g}"], tag=f"gT{g}")
            def e4(eng, g=g, bk2=bk2):
                return eng.copy(g_fm[:, g * 128:(g + 1) * 128], PB[bk2][0:64, 0:128])
            sop(e4, reads=[f"pb{bk2}"], writes=["g_fm"], tag=f"gE{g}")
        dma(gdram, g_fm[:, :], "gdram", reads=["g_fm"], writes=["gdram"], tag=f"gW{t}")
        for e_i in range(NE):
            sl = e_i % 2
            if e_i % 2 == 0:
                gsl = (e_i // 2) % 2
                gsrc = AP(tensor=gdram.tensor, offset=e_i * TILE, ap=[[1, 1], [1, 2 * TILE]])
                dma(g8[gsl][0:1, :], gsrc, f"g8_{gsl}", reads=["gdram"],
                    writes=[f"g8_{gsl}"], tag=f"g8L{e_i}")
            hbk = [2, 3][e_i % 2]
            for ic in range(FC):
                mm(hbk, PB[hbk][:, 0:TILE], ke_sb[sl][:, ic * 128:(ic + 1) * 128],
                   ln2_fm[ic][:, :],
                   reads=[f"ke{sl}", f"lnf{ic}"], start=(ic == 0), stop=(ic == FC - 1),
                   tag=f"h{e_i}i{ic}")
            if e_i % 2 == 0:
                gbk = [6, 7][(e_i // 2) % 2]
                gsl = (e_i // 2) % 2
                gpr = (e_i // 2) % 2
                mm(gbk, PB[gbk][:, 0:2 * TILE], consts["ones_row"][:, :],
                   g8[gsl][0:1, 0:2 * TILE],
                   reads=["ones_row_sb", f"g8_{gsl}"], tag=f"gb{e_i}")
                def eg(eng, gpr=gpr, gbk=gbk):
                    return eng.copy(gb_sb[gpr][:, :], PB[gbk][:, 0:2 * TILE])
                sop(eg, reads=[f"pb{gbk}"], writes=[f"gb{gpr}"], tag=f"gbE{e_i}")
            if e_i + 2 < NE:
                e_n = e_i + 2
                sln = e_n % 2
                ksrc = AP(tensor=keys_d.tensor, offset=e_n * D * 128,
                          ap=[[128, 128], [128 * 128, FC], [1, 128]])
                dma(ke_sb[sln][:, :], ksrc, f"ke{sln}", writes=[f"ke{sln}"], tag=f"keL{e_n}")
            gpr = (e_i // 2) % 2
            def es(eng, sl=sl, hbk=hbk, gpr=gpr, e_i=e_i):
                return eng.scalar_tensor_tensor(sc_sb[sl][:, :], PB[hbk][:, 0:TILE], 0.0,
                                                gb_sb[gpr][:, (e_i % 2) * TILE:(e_i % 2 + 1) * TILE],
                                                ALU.max, ALU.mult)
            vop(es, reads=[f"pb{hbk}", f"gb{gpr}"], writes=[f"sc{sl}"], tag=f"sc{e_i}")
            def emit_out2(e_j):
                slj = e_j % 2
                for g in range(G):
                    for half in range(2):
                        ob = OUT2B[(g, half)]
                        mm(ob, PB[ob][:, 0:512], sc_sb[slj][:, g * 128:(g + 1) * 128],
                           ve_sb[slj][:, half * 512:(half + 1) * 512],
                           reads=[f"sc{slj}", f"ve{slj}"], start=(e_j == 0), stop=(e_j == NE - 1),
                           tag=f"o2_{e_j}g{g}h{half}")
            if e_i >= 1:
                emit_out2(e_i - 1)
            if 2 <= e_i + 1 < NE:
                e_v = e_i + 1
                slv = e_v % 2
                dma(ve_sb[slv][:, :], values_d[e_v, :, :], f"ve{slv}", writes=[f"ve{slv}"], tag=f"veL{e_v}")
            if e_i == NE - 1:
                emit_out2(e_i)
        for g in range(G):
            for half in range(2):
                ob = OUT2B[(g, half)]
                def e(eng, g=g, half=half, ob=ob):
                    return eng.tensor_add(x_tok[g][:, half * 512:(half + 1) * 512],
                                          x_tok[g][:, half * 512:(half + 1) * 512],
                                          PB[ob][:, 0:512])
                vop(e, reads=[f"pb{ob}", f"x_tok{g}"], writes=[f"x_tok{g}"], tag=f"res2{g}{half}")
            r0 = t * TILE + g * 128
            dma(out_d[r0:r0 + 128, :], x_tok[g][:, :], "out",
                reads=[f"x_tok{g}"], tag=f"st{t}{g}")

    return nc, B, ctx


_CACHE = {}

def _get_program():
    if "nc" in _CACHE:
        return _CACHE["nc"]
    nc, B, ctx = build(NT=16)
    B.finalize(nc, None)
    sems = {}
    for e in ENGINES:
        sems[("eng", e)] = ctx.enter_context(nc.semaphore(f"sem_{e}"))
    for s in B.stream_totals:
        sems[("dma", s)] = ctx.enter_context(nc.semaphore(f"dma_{s}"))
    with nc.Block() as block:
        @block.sync
        def _(eng):
            B.emit_engine("sync", eng, sems)
            for s, c in B.stream_totals.items():
                eng.wait_ge(sems[("dma", s)], 16 * c)
        @block.tensor
        def _(eng):
            B.emit_engine("tensor", eng, sems)
        @block.vector
        def _(eng):
            B.emit_engine("vector", eng, sems)
        @block.scalar
        def _(eng):
            B.emit_engine("scalar", eng, sems)
        @block.gpsimd
        def _(eng):
            B.emit_engine("gpsimd", eng, sems)
    ctx.close()
    _CACHE["nc"] = nc
    return nc


def kernel(**inputs):
    from concourse.bass_utils import run_bass_kernel_spmd
    nc = _get_program()
    in_maps = [host_inputs(inputs, core) for core in range(8)]
    res = run_bass_kernel_spmd(nc, in_maps, core_ids=list(range(8)))
    out = np.concatenate([res.results[c]["out"].reshape(2, 2048, 1024) for c in range(8)], axis=0)
    return out


# version 13 (final)
# speedup vs baseline: 1.2714x; 1.2216x over previous
import numpy as np
import concourse.bass as bass
import concourse.mybir as mybir

F32 = mybir.dt.float32
AF = mybir.ActivationFunctionType
ALU = mybir.AluOpType

ENGINES = ("sync", "tensor", "vector", "scalar", "gpsimd")


class _Op:
    __slots__ = ("idx", "engine", "emit", "deps", "is_dma", "dma_stream",
                 "sig_idx", "waits", "tag", "drain_before")
    def __init__(self, idx, engine, emit, deps, is_dma, dma_stream, tag):
        self.idx = idx
        self.engine = engine
        self.emit = emit
        self.deps = deps
        self.is_dma = is_dma
        self.dma_stream = dma_stream
        self.sig_idx = None
        self.waits = None
        self.tag = tag
        self.drain_before = False


class Builder:
    def __init__(self):
        self.ops = []
        self.bufs = {}
        self.dma_streams = {}

    def _buf(self, name):
        if name not in self.bufs:
            self.bufs[name] = {"w": None, "r": []}
        return self.bufs[name]

    def op(self, engine, emit, reads=(), writes=(), tag="", dma_stream=None):
        assert engine in ENGINES, engine
        deps = set()
        for b in reads:
            e = self._buf(b)
            if e["w"] is not None:
                deps.add(e["w"])
        for b in writes:
            e = self._buf(b)
            if e["w"] is not None:
                deps.add(e["w"])
            deps.update(e["r"])
        o = _Op(len(self.ops), engine, emit, deps, dma_stream is not None, dma_stream, tag)
        self.ops.append(o)
        for b in reads:
            self._buf(b)["r"].append(o)
        for b in writes:
            self.bufs[b] = {"w": o, "r": []}
        return o

    def dma(self, out_ap, in_ap, stream, reads=(), writes=(), tag="", engine="sync"):
        def emit(eng):
            return eng.dma_start(out=out_ap, in_=in_ap)
        return self.op(engine, emit, reads=reads, writes=writes, tag=tag,
                       dma_stream=stream)

    def finalize(self, nc, block, final_engine="sync"):
        needs_sig = set()
        for o in self.ops:
            for d in o.deps:
                if d.is_dma or d.engine != o.engine:
                    needs_sig.add(d)
        for o in self.ops:
            if o.is_dma:
                needs_sig.add(o)
        eng_count = {e: 0 for e in ENGINES}
        stream_count = {}
        for o in self.ops:
            if o not in needs_sig:
                continue
            if o.is_dma:
                c = stream_count.get(o.dma_stream, 0) + 1
                stream_count[o.dma_stream] = c
                o.sig_idx = c * 16
            else:
                c = eng_count[o.engine] + 1
                eng_count[o.engine] = c
                o.sig_idx = c
        watermark = {}
        for o in self.ops:
            waits = {}
            for d in o.deps:
                if d.is_dma:
                    key = ("dma", d.dma_stream)
                elif d.engine != o.engine:
                    key = ("eng", d.engine)
                else:
                    continue
                assert d.sig_idx is not None
                if waits.get(key, 0) < d.sig_idx:
                    waits[key] = d.sig_idx
            lst = []
            for key, v in waits.items():
                wm = watermark.get((o.engine, key), 0)
                if v > wm:
                    watermark[(o.engine, key)] = v
                    lst.append((key, v))
            o.waits = lst
        last_drain = -1
        for o in self.ops:
            if o.engine != "vector" or o.is_dma:
                continue
            mx = -1
            for d in o.deps:
                if (not d.is_dma) and d.engine == "vector":
                    mx = max(mx, d.idx)
            if mx > last_drain:
                o.drain_before = True
                last_drain = o.idx
        self.stream_totals = stream_count
        return eng_count, stream_count

    def emit_engine(self, engname, engproxy, sems):
        for o in self.ops:
            if o.engine != engname:
                continue
            if o.drain_before:
                engproxy.drain()
            for key, v in o.waits:
                engproxy.wait_ge(sems[key], v)
            inst = o.emit(engproxy)
            if o.sig_idx is not None:
                if o.is_dma:
                    inst.then_inc(sems[("dma", o.dma_stream)], 16)
                else:
                    inst.then_inc(sems[("eng", engname)], 1)




"""Megakernel program builder v2: 2-way pipelined attention instances,
overlapped weight streaming. Residual token-major; delta rule chunk=128 via
exact (I-B)^-1 = prod(I+B^{2^k}); all matmuls fp32."""
import numpy as np
from contextlib import ExitStack
import concourse.bass as bass
import concourse.mybir as mybir
from concourse.bass_types import AP

F32 = mybir.dt.float32
AF = mybir.ActivationFunctionType
ALU = mybir.AluOpType
AX = mybir.AxisListType

TILE = 256
G = 2
H = 8
FC = 8
D = 1024
NE = 64
NLEV = 7


def host_constants():
    i_idx = np.arange(128)[:, None]
    j_idx = np.arange(128)[None, :]
    return {
        "ident": np.eye(128, dtype=np.float32),
        "m_low_neg": np.where(i_idx > j_idx, -1.0, 0.0).astype(np.float32),
        "m_up_neg": np.where(i_idx < j_idx, -1.0, 0.0).astype(np.float32),
        "m_upincl": np.where(i_idx <= j_idx, 1.0, 0.0).astype(np.float32),
        "ones_col": np.ones((128, 1), np.float32),
        "ones_row": np.ones((1, 128), np.float32),
    }


def host_inputs(full, core):
    x = np.asarray(full["L0_attn_hidden_states"])
    xs = np.ascontiguousarray(x[2 * core:2 * core + 2].reshape(4096, D))
    conv = np.stack([np.asarray(full["L0_attn_conv_q_w"]),
                     np.asarray(full["L0_attn_conv_k_w"]),
                     np.asarray(full["L0_attn_conv_v_w"])])
    conv = conv.reshape(3, FC, 128, 4).transpose(2, 0, 1, 3).reshape(128, 96)
    bmix = np.concatenate([np.asarray(full["L0_attn_b_w"]),
                           np.asarray(full["L0_attn_mix_w"])], axis=1)
    m = {
        "x_in": xs,
        "wq": np.asarray(full["L0_attn_q_w"]),
        "wk": np.asarray(full["L0_attn_k_w"]),
        "wv": np.asarray(full["L0_attn_v_w"]),
        "wo": np.asarray(full["L0_attn_o_w"]),
        "convw": np.ascontiguousarray(conv),
        "bmix_w": np.ascontiguousarray(bmix),
        "mix_b": np.asarray(full["L0_attn_mix_b"]).reshape(1, 8),
        "mix_bias": np.asarray(full["L0_attn_mix_bias"]).reshape(1, 8),
        "onorm": np.asarray(full["L0_attn_o_norm_w"]).reshape(128, 1),
        "sel_r": np.ascontiguousarray(np.asarray(full["L0_ffn_expert_sel"]).T),
        "keys": np.asarray(full["L0_ffn_keys"]),
        "values": np.asarray(full["L0_ffn_values"]),
    }
    m.update(host_constants())
    return {k: np.ascontiguousarray(v, dtype=np.float32) for k, v in m.items()}


def build(NT=16, tiles_per_seq=8):
    nc = bass.Bass("TRN2", target_bir_lowering=False)
    B = Builder()
    ctx = ExitStack()

    dram_in = {}
    def din(name, shape):
        dram_in[name] = nc.dram_tensor(name, shape, F32, kind="ExternalInput").ap()
        return dram_in[name]

    x_in = din("x_in", [4096, D])
    wq = din("wq", [D, D]); wk = din("wk", [D, D]); wv = din("wv", [D, D])
    wo = din("wo", [D, D])
    convw_d = din("convw", [128, 96])
    bmix_d = din("bmix_w", [D, 16])
    mixb_d = din("mix_b", [1, 8]); mixbias_d = din("mix_bias", [1, 8])
    onorm_d = din("onorm", [128, 1])
    sel_d = din("sel_r", [D, NE])
    keys_d = din("keys", [NE, D, 128])
    values_d = din("values", [NE, 128, D])
    for cn in ("ident", "m_low_neg", "m_up_neg", "m_upincl"):
        din(cn, [128, 128])
    din("ones_col", [128, 1]); din("ones_row", [1, 128])

    out_d = nc.dram_tensor("out", [4096, D], F32, kind="ExternalOutput").ap()
    gdram = nc.dram_tensor("g_dram", [NE, TILE], F32, kind="Internal").ap()

    sbufs = {}
    def sb(name, shape):
        h = ctx.enter_context(nc.sbuf_tensor(name, shape, F32))
        sbufs[name] = h
        return h

    wbuf = [sb(f"wbuf{i}", [128, FC * 512]) for i in range(2)]
    wo_sb = [sb(f"wo{h}", [128, D]) for h in range(H)]
    sel_sb = sb("sel_sb", [128, FC * NE])
    ke_sb = [sb(f"ke{i}", [128, D]) for i in range(2)]
    ve_sb = [sb(f"ve{i}", [128, D]) for i in range(2)]
    x_tok = [sb(f"x_tok{g}", [128, D]) for g in range(G)]
    lnt = [sb(f"lnt{g}", [128, D]) for g in range(G)]
    ln1_fm = [sb(f"lnf{c}", [128, TILE]) for c in range(FC)]
    ln2_fm = ln1_fm
    pre = {p: [sb(f"{p}_pre{c}", [128, TILE + 3]) for c in range(FC)] for p in "qkv"}
    fm = {p: [pre[p][c][:, 3:3 + TILE] for c in range(FC)] for p in "qkv"}
    tails = {p: sb(f"{p}_tail", [128, 3 * FC]) for p in "qkv"}
    cv = {p: sb(f"cv_{p}", [128, TILE]) for p in "qkv"}
    convw = sb("convw_sb", [128, 96])
    bmix_sb = sb("bmix_sb", [128, FC * 16])
    biasrow = sb("biasrow", [1, 16])
    tmp8 = sb("tmp8", [1, 8])
    onorm_sb = sb("onorm_sb", [128, 1])
    bm = [sb(f"bm{g}", [128, 16]) for g in range(G)]
    sqb = [sb(f"sqb{g}", [128, 8]) for g in range(G)]
    S_sb = [sb(f"S{h}", [128, 128]) for h in range(H)]
    consts = {cn: sb(cn + "_sb", list(np.shape(host_constants()[cn])))
              for cn in ("ident", "m_low_neg", "m_up_neg", "m_upincl",
                          "ones_col", "ones_row")}
    inst = []
    for pn in range(4):
        dd = {}
        names = [("k_tok", [128, 128]), ("uw0", [128, 256]),
                 ("ktf", [128, 128]),
                 ("attnT", [128, 128]), ("u_p", [128, 128]),
                 ("Ma", [128, 128]), ("sq_i", [128, 128]),
                 ("kss", [128, 1]), ("kss2", [128, 1]), ("rsqk", [128, 1]),
                 ("nwa", [128, 1]), ("nwb", [128, 1])]
        if pn < 3:
            names += [("ofm", [128, 128]), ("oss", [128, 1]), ("oss2", [128, 1]),
                      ("rms_b", [128, 1])]
        for nm, shp in names:
            dd[nm] = sb(f"{nm}_{pn}", shp)
        dd["P"] = [sb(f"P{k}_{pn}", [128, 128]) for k in range(2)]
        dd["PT"] = [sb(f"PT{k}_{pn}", [128, 128]) for k in range(2)]
        dd["_p"] = pn
        inst.append(dd)
    nw1 = sb("nw1", [128, 1]); nw2 = sb("nw2", [128, 1])
    qhf_t = [sb(f"qhf_t{h}", [128, TILE]) for h in range(H)]
    v_tok_t = [sb(f"v_tok_t{h}", [128, 128]) for h in range(H)]
    o_raw = [sb(f"o_raw{h}", [128, 128]) for h in range(H)]
    qsq_t = [sb(f"qsq_t{h}", [128, TILE]) for h in range(2)]
    qrow_t = [sb(f"qrow_t{h}", [1, TILE]) for h in range(2)]
    rsqq_t = [sb(f"rsqq_t{h}", [1, TILE]) for h in range(2)]
    nra_t = [sb(f"nra_t{h}", [1, TILE]) for h in range(2)]
    ml = [sb(f"ml{g}", [128, 1]) for g in range(G)]
    rl = [sb(f"rl{g}", [128, 1]) for g in range(G)]
    ssl = sb("ssl", [128, 1]); v1l = sb("v1l", [128, 1]); m2l = sb("m2l", [128, 1])
    sgate = [sb(f"sgate{g}", [128, NE]) for g in range(G)]
    top8 = sb("top8", [128, 8]); gmask = sb("gmask", [128, NE])
    g_fm = sb("g_fm", [64, TILE])
    sc_sb = [sb(f"sc{i}", [128, TILE]) for i in range(2)]
    gb_sb = [sb(f"gb{i}", [128, 2 * TILE]) for i in range(2)]
    g8 = [sb(f"g8_{i}", [1, 2 * TILE]) for i in range(2)]

    psum = [ctx.enter_context(nc.psum_tensor(f"pb{i}", [128, 512], F32)) for i in range(8)]
    PB = [p.ap() for p in psum]

    AOB = [4, 5]
    OUT2B = {(0, 0): 4, (0, 1): 5, (1, 0): 0, (1, 1): 1}
    _rot = {"tp": 0, "mm": 0, "small": 0}
    def rot(kind):
        lst = {"tp": [0, 1], "mm": [2, 3], "small": [6, 7]}[kind]
        i = _rot[kind] % len(lst); _rot[kind] += 1
        return lst[i]

    def dma(out_ap, in_ap, stream, reads=(), writes=(), tag=""):
        return B.dma(out_ap, in_ap, stream, reads=reads, writes=writes, tag=tag)

    def mm(bank, out_ap, lhsT, rhs, reads, start=True, stop=True, tag=""):
        def e(eng, out_ap=out_ap, lhsT=lhsT, rhs=rhs, start=start, stop=stop):
            return eng.matmul(out_ap, lhsT, rhs, start=start, stop=stop)
        return B.op("tensor", e, reads=list(reads), writes=[f"pb{bank}"], tag=tag)

    def tr(bank, out_ap, in_ap, reads, tag=""):
        def e(eng, out_ap=out_ap, in_ap=in_ap):
            return eng.transpose(out_ap, in_ap, consts["ident"][:, :])
        return B.op("tensor", e, reads=list(reads) + ["ident_sb"], writes=[f"pb{bank}"], tag=tag)

    def vop(emit, reads, writes, tag=""):
        return B.op("vector", emit, reads=reads, writes=writes, tag=tag)
    def sop(emit, reads, writes, tag=""):
        return B.op("scalar", emit, reads=reads, writes=writes, tag=tag)
    def gop(emit, reads, writes, tag=""):
        return B.op("gpsimd", emit, reads=reads, writes=writes, tag=tag)

    def emit_rsqrt(v_ap, vname, out_ap, outname, t1_ap, t1n, t2_ap, t2n, tag):
        vop(lambda e: e.reciprocal(t1_ap, v_ap), reads=[vname], writes=[t1n], tag=tag + "_rc")
        sop(lambda e: e.activation(t2_ap, t1_ap, AF.Sqrt), reads=[t1n], writes=[t2n], tag=tag + "_sq")
        gop(lambda e: e.tensor_mul(t1_ap, t2_ap, t2_ap), reads=[t2n], writes=[t1n], tag=tag + "_n1")
        gop(lambda e: e.tensor_mul(t1_ap, t1_ap, v_ap), reads=[t1n, vname], writes=[t1n], tag=tag + "_n2")
        gop(lambda e: e.tensor_scalar(t1_ap, t1_ap, -0.5, 1.5, ALU.mult, ALU.add), reads=[t1n], writes=[t1n], tag=tag + "_n3")
        gop(lambda e: e.tensor_mul(out_ap, t2_ap, t1_ap), reads=[t1n, t2n], writes=[outname], tag=tag + "_n4")

    def ln_token_major(g, t):
        xn, lname = f"x_tok{g}", f"lnt{g}"
        mname, rname = f"m{g}", f"r{g}"
        x_ap = x_tok[g][:, :]; ln_ap = lnt[g][:, :]
        m_t, r_t = ml[g], rl[g]
        vop(lambda e: e.reduce_sum(m_t[:, :], x_ap, axis=AX.X), reads=[xn], writes=[mname], tag=f"lnsum{t}{g}")
        vop(lambda e: e.tensor_scalar(m_t[:, :], m_t[:, :], 1.0 / 1024.0, None, ALU.mult),
            reads=[mname], writes=[mname], tag=f"lnm{t}{g}")
        sop(lambda e: e.activation(ln_ap, x_ap, AF.Square, accum_out=ssl[:, :]),
            reads=[xn], writes=["ssl", lname], tag=f"lnsq{t}{g}")
        vop(lambda e: e.tensor_scalar(v1l[:, :], ssl[:, :], 1.0 / 1024.0, 1e-5, ALU.mult, ALU.add),
            reads=["ssl"], writes=["v1l"], tag=f"lnv1{t}{g}")
        vop(lambda e: e.tensor_mul(m2l[:, :], m_t[:, :], m_t[:, :]), reads=[mname], writes=["m2l"], tag=f"lnm2{t}{g}")
        vop(lambda e: e.tensor_sub(v1l[:, :], v1l[:, :], m2l[:, :]), reads=["v1l", "m2l"], writes=["v1l"], tag=f"lnv{t}{g}")
        emit_rsqrt(v1l[:, :], "v1l", r_t[:, :], rname, nw1[:, :], "nw1", nw2[:, :], "nw2", f"lnr{t}{g}")
        vop(lambda e: e.tensor_scalar(ln_ap, x_ap, m_t[:, :], r_t[:, :], ALU.subtract, ALU.mult),
            reads=[xn, mname, rname], writes=[lname], tag=f"lnn{t}{g}")

    def attn_instance(t, g, h):
        d = inst[h % 4]
        pn = d["_p"]
        def N(nm):
            return f"{nm}_{pn}"
        mmb = {0: [0, 2], 1: [1, 3], 2: [6, 7], 3: [4, 5]}[pn]
        mi = [0]
        def mrot():
            b = mmb[mi[0] % 2]; mi[0] += 1
            return b
        qs = fm["q"][h][:, g * 128:(g + 1) * 128]
        ks = fm["k"][h][:, g * 128:(g + 1) * 128]
        vs = fm["v"][h][:, g * 128:(g + 1) * 128]
        kn, qn, vn = f"k_pre{h}", f"q_pre{h}", f"v_pre{h}"
        k_tok = d["k_tok"]; uw0 = d["uw0"]; uw_s = d["uw0"]
        ktf = d["ktf"]; wtf = d["ktf"]; attnT = d["attnT"]; u_p = d["u_p"]
        qhf = qhf_t[h][:, g * 128:(g + 1) * 128]
        Ma = d["Ma"]
        P = d["P"]; PT = d["PT"]; sq_i = d["sq_i"]
        sqb_col = sqb[g][:, h:h + 1]
        gcol = bm[g][:, 8 + h:9 + h]
        Sn = f"S{h}"

        b_ = mrot()
        tr(b_, PB[b_][:, 0:128], ks, reads=[kn], tag=f"kT{t}{g}{h}")
        sop(lambda e, b_=b_: e.copy(k_tok[:, :], PB[b_][:, 0:128]), reads=[f"pb{b_}"], writes=[N("k_tok")], tag="kTe")
        sop(lambda e: e.activation(sq_i[:, :], k_tok[:, :], AF.Square, accum_out=d["kss"][:, :]),
            reads=[N("k_tok")], writes=[N("kss"), N("sq_i")], tag="ksq")
        vop(lambda e: e.tensor_scalar(d["kss2"][:, :], d["kss"][:, :], 1e-12, None, ALU.add),
            reads=[N("kss")], writes=[N("kss2")], tag="keps")
        emit_rsqrt(d["kss2"][:, :], N("kss2"), d["rsqk"][:, :], N("rsqk"),
                   d["nwa"][:, :], N("nwa"), d["nwb"][:, :], N("nwb"), f"krsq{pn}")
        yield
        b_ = mrot()
        tr(b_, PB[b_][:, 0:128], vs, reads=[vn], tag=f"vT{t}{g}{h}")
        sop(lambda e, b_=b_: e.copy(v_tok_t[h][:, :], PB[b_][:, 0:128]), reads=[f"pb{b_}"], writes=[f"v_tok_t{h}"], tag="vTe")
        vop(lambda e: e.tensor_scalar(k_tok[:, :], k_tok[:, :], d["rsqk"][:, :], None, ALU.mult),
            reads=[N("k_tok"), N("rsqk")], writes=[N("k_tok")], tag="khat")
        vop(lambda e: e.tensor_scalar(uw0[:, 128:256], k_tok[:, :], sqb_col, None, ALU.mult),
            reads=[N("k_tok"), f"sqb{g}"], writes=[N("uw0")], tag="ktil")
        vop(lambda e: e.tensor_scalar(uw0[:, 0:128], v_tok_t[h][:, :], sqb_col, None, ALU.mult),
            reads=[f"v_tok_t{h}", f"sqb{g}"], writes=[N("uw0")], tag="vtil")
        yield
        b_ = mrot()
        tr(b_, PB[b_][:, 0:128], uw0[:, 128:256], reads=[N("uw0")], tag="ktfT")
        sop(lambda e, b_=b_: e.copy(ktf[:, :], PB[b_][:, 0:128]), reads=[f"pb{b_}"], writes=[N("ktf")], tag="ktfE")
        yield
        b0 = mrot()
        mm(b0, PB[b0][:, 0:128], ktf[:, :], ktf[:, :], reads=[N("ktf")], tag="Gt")
        def e(eng, b0=b0):
            return eng.tensor_tensor(P[0][:, :], PB[b0][:, 0:128], consts["m_low_neg"][:, :], ALU.mult)
        vop(e, reads=[f"pb{b0}", "m_low_neg_sb"], writes=[N("P0")], tag="Bev")
        def e(eng, b0=b0):
            return eng.tensor_tensor(PT[0][:, :], PB[b0][:, 0:128], consts["m_up_neg"][:, :], ALU.mult)
        vop(e, reads=[f"pb{b0}", "m_up_neg_sb"], writes=[N("PT0")], tag="BTev")
        vop(lambda e: e.tensor_add(Ma[:, :], PT[0][:, :], consts["ident"][:, :]),
            reads=[N("PT0"), "ident_sb"], writes=[N("Ma")], tag="M1")
        yield
        Man = N("Ma")
        for k in range(1, NLEV):
            sl_ = k % 2; slp = (k - 1) % 2
            bk = mrot()
            mm(bk, PB[bk][:, 0:128], PT[slp][:, :], P[slp][:, :],
               reads=[N(f"PT{slp}"), N(f"P{slp}")], tag=f"sqP{k}")
            ep = sop if k % 2 == 0 else vop
            def e(eng, sl_=sl_, bk=bk):
                if hasattr(eng, "copy"):
                    return eng.copy(P[sl_][:, :], PB[bk][:, 0:128])
                return eng.tensor_copy(P[sl_][:, :], PB[bk][:, 0:128])
            ep(e, reads=[f"pb{bk}"], writes=[N(f"P{sl_}")], tag=f"sqPe{k}")
            if k < NLEV - 1:
                bk2 = mrot()
                mm(bk2, PB[bk2][:, 0:128], P[slp][:, :], PT[slp][:, :],
                   reads=[N(f"P{slp}"), N(f"PT{slp}")], tag=f"sqT{k}")
                ep2 = vop if k % 2 == 0 else sop
                def e(eng, sl_=sl_, bk2=bk2):
                    if hasattr(eng, "copy"):
                        return eng.copy(PT[sl_][:, :], PB[bk2][:, 0:128])
                    return eng.tensor_copy(PT[sl_][:, :], PB[bk2][:, 0:128])
                ep2(e, reads=[f"pb{bk2}"], writes=[N(f"PT{sl_}")], tag=f"sqTe{k}")
            yield
            bk3 = mrot()
            mm(bk3, PB[bk3][:, 0:128], P[sl_][:, :], Ma[:, :],
               reads=[N(f"P{sl_}"), Man], tag=f"ap{k}")
            def e(eng, bk3=bk3):
                return eng.tensor_add(Ma[:, :], Ma[:, :], PB[bk3][:, 0:128])
            vop(e, reads=[f"pb{bk3}", Man], writes=[Man], tag=f"apE{k}")
            yield
        bk = mrot()
        mm(bk, PB[bk][:, 0:256], Ma[:, :], uw0[:, :], reads=[Man, N("uw0")], tag="uwmm")
        def e(eng, bk=bk):
            return eng.tensor_scalar(uw_s[:, :], PB[bk][:, 0:256], sqb_col, None, ALU.mult)
        vop(e, reads=[f"pb{bk}", f"sqb{g}"], writes=[N("uw0")], tag="uwE")
        yield
        b_ = mrot()
        tr(b_, PB[b_][:, 0:128], uw_s[:, 128:256], reads=[N("uw0")], tag="wT")
        sop(lambda e, b_=b_: e.copy(wtf[:, :], PB[b_][:, 0:128]), reads=[f"pb{b_}"], writes=[N("ktf")], tag="wTe")
        yield
        bk = mrot()
        mm(bk, PB[bk][:, 0:128], wtf[:, :], S_sb[h][:, :], reads=[N("ktf"), Sn], tag="wS")
        def e(eng, bk=bk):
            return eng.tensor_sub(u_p[:, :], uw_s[:, 0:128], PB[bk][:, 0:128])
        vop(e, reads=[f"pb{bk}", N("uw0")], writes=[N("u_p")], tag="uprime")
        bk = mrot()
        mm(bk, PB[bk][:, 0:128], ks, qhf, reads=[kn, f"qhf_t{h}"], tag="aT")
        def e(eng, bk=bk):
            return eng.scalar_tensor_tensor(attnT[:, :], PB[bk][:, 0:128], d["rsqk"][:, :],
                                            consts["m_upincl"][:, :], ALU.mult, ALU.mult)
        vop(e, reads=[f"pb{bk}", N("rsqk"), "m_upincl_sb"], writes=[N("attnT")], tag="aTe")
        yield
        bk = mrot()
        mm(bk, PB[bk][:, 0:128], qhf, S_sb[h][:, :], reads=[f"qhf_t{h}", Sn], start=True, stop=False, tag="qS")
        mm(bk, PB[bk][:, 0:128], attnT[:, :], u_p[:, :], reads=[N("attnT"), N("u_p")], start=False, stop=True, tag="au")
        def e(eng, bk=bk):
            return eng.tensor_sub(o_raw[h][:, :], PB[bk][:, 0:128], v_tok_t[h][:, :])
        vop(e, reads=[f"pb{bk}", f"v_tok_t{h}"], writes=[f"o_raw{h}"], tag="gsub")
        bk = mrot()
        mm(bk, PB[bk][:, 0:128], k_tok[:, :], u_p[:, :], reads=[N("k_tok"), N("u_p")], tag="Sup")
        def e(eng, bk=bk):
            return eng.tensor_add(S_sb[h][:, :], S_sb[h][:, :], PB[bk][:, 0:128])
        vop(e, reads=[f"pb{bk}", Sn], writes=[Sn], tag="SupE")

    dma(convw[:, :], convw_d, "convw", writes=["convw_sb"])
    dma(biasrow[0:1, 8:16], mixb_d, "biasrow", writes=["biasrow"])
    dma(tmp8[:, :], mixbias_d, "tmp8", writes=["tmp8"])
    dma(onorm_sb[:, :], onorm_d, "onorm", writes=["onorm_sb"])
    for cn in consts:
        nm = cn + "_sb"
        dma(sbufs[nm][:, :], dram_in[cn], nm, writes=[nm])
    sel_src = AP(tensor=sel_d.tensor, offset=0, ap=[[NE, 128], [128 * NE, FC], [1, NE]])
    dma(sel_sb[:, :], sel_src, "sel_sb", writes=["sel_sb"])
    for h in range(H):
        dma(wo_sb[h][:, :], wo[h * 128:(h + 1) * 128, :], f"wo{h}", writes=[f"wo{h}"])
    gop(lambda e: e.memset(biasrow[0:1, 0:8], 0.0), reads=["biasrow"], writes=["biasrow"], tag="biasz")
    vop(lambda e: e.tensor_add(biasrow[0:1, 8:16], biasrow[0:1, 8:16], tmp8[0:1, :]),
        reads=["biasrow", "tmp8"], writes=["biasrow"], tag="biasadd")
    for h in range(H):
        def e(eng, h=h):
            return eng.tensor_scalar(wo_sb[h][:, :], wo_sb[h][:, :], onorm_sb[:, :], None, ALU.mult)
        vop(e, reads=[f"wo{h}", "onorm_sb"], writes=[f"wo{h}"], tag="wo_scale")
    bmix_src = AP(tensor=bmix_d.tensor, offset=0, ap=[[16, 128], [128 * 16, FC], [1, 16]])
    dma(bmix_sb[:, :], bmix_src, "bmix_sb", writes=["bmix_sb"])

    for t in range(NT):
        seq_start = (t % tiles_per_seq) == 0
        for g in range(G):
            r0 = t * TILE + g * 128
            dma(x_tok[g][:, :], x_in[r0:r0 + 128, :], f"x_tok{g}",
                writes=[f"x_tok{g}"], tag=f"xin{t}g{g}")
        for g in range(G):
            ln_token_major(g, t)
            for c in range(FC):
                bk = rot("tp")
                tr(bk, PB[bk][:, 0:128], lnt[g][:, c * 128:(c + 1) * 128],
                   reads=[f"lnt{g}"], tag=f"ln1T{t}{g}{c}")
                def e(eng, c=c, g=g, bk=bk):
                    return eng.copy(ln1_fm[c][:, g * 128:(g + 1) * 128], PB[bk][:, 0:128])
                sop(e, reads=[f"pb{bk}"], writes=[f"lnf{c}"], tag=f"ln1E{t}{g}{c}")
        hoist_queue = []
        def _hoist_unit_list():
            units = []
            for h_ in range(H):
                units.append(lambda h_=h_: (_emit_qss(h_), (_emit_qsquare(h_ + 2) if h_ + 2 < H else None)))
                if h_ >= 1:
                    units.append(lambda h_=h_: _emit_qbc(h_ - 1))
            units.append(lambda: _emit_qbc(H - 1))
            return units
        def _emit_qbc(h):
            hp2 = h % 2
            qfm_h = fm["q"][h]
            qnh = f"q_pre{h}"
            bk2 = rot("small")
            mm(bk2, PB[bk2][:, 0:TILE], consts["ones_row"][:, :], rsqq_t[hp2][0:1, :],
               reads=["ones_row_sb", f"rsqq_t{hp2}"], tag=f"qbcH{h}")
            def e(eng, bk2=bk2, qfm_h=qfm_h, h=h):
                return eng.tensor_tensor(qhf_t[h][:, :], qfm_h, PB[bk2][:, 0:TILE], ALU.mult)
            vop(e, reads=[qnh, f"pb{bk2}"], writes=[f"qhf_t{h}"], tag=f"qhatH{h}")
        def _emit_qsquare(h):
            hp2 = h % 2
            qfm_h = fm["q"][h]
            qnh = f"q_pre{h}"
            sop(lambda e, qfm_h=qfm_h, hp2=hp2: e.activation(qsq_t[hp2][:, :], qfm_h, AF.Square),
                reads=[qnh], writes=[f"qsq_t{hp2}"], tag=f"qsqH{h}")
        def _emit_qss(h):
            hp2 = h % 2
            qfm_h = fm["q"][h]
            qnh = f"q_pre{h}"
            bk = rot("small")
            mm(bk, PB[bk][0:1, 0:TILE], consts["ones_col"][:, :], qsq_t[hp2][:, :],
               reads=["ones_col_sb", f"qsq_t{hp2}"], tag=f"qssH{h}")
            def e(eng, bk=bk, hp2=hp2):
                return eng.tensor_scalar(qrow_t[hp2][0:1, :], PB[bk][0:1, 0:TILE], 1e-12, None, ALU.add)
            vop(e, reads=[f"pb{bk}"], writes=[f"qrow_t{hp2}"], tag=f"qepsH{h}")
            emit_rsqrt(qrow_t[hp2][0:1, :], f"qrow_t{hp2}", rsqq_t[hp2][0:1, :], f"rsqq_t{hp2}",
                       nra_t[hp2][0:1, :], f"nra_t{hp2}", rsqq_t[hp2][0:1, :], f"rsqq_t{hp2}", f"qrsqH{h}")
        for pi, (p, wd) in enumerate((("q", wq), ("k", wk), ("v", wv))):
            for c in range(FC):
                if seq_start:
                    gop(lambda e, p=p, c=c: e.memset(pre[p][c][:, 0:3], 0.0),
                        reads=[], writes=[f"{p}_pre{c}"], tag=f"tz{p}{c}")
                else:
                    gop(lambda e, p=p, c=c: e.tensor_copy(pre[p][c][:, 0:3], tails[p][:, c * 3:c * 3 + 3]),
                        reads=[f"{p}_tail"], writes=[f"{p}_pre{c}"], tag=f"tc{p}{c}")
            for hf in range(2):
                wslot = (pi * 2 + hf) % 2
                wsrc = AP(tensor=wd.tensor, offset=hf * 512,
                          ap=[[D, 128], [128 * D, FC], [1, 512]])
                dma(wbuf[wslot][:, :], wsrc, f"wbuf{wslot}", writes=[f"wbuf{wslot}"], tag=f"w{p}{t}h{hf}")
                for oc4 in range(4):
                    oc = hf * 4 + oc4
                    bk = rot("mm")
                    for ic in range(FC):
                        mm(bk, PB[bk][:, 0:TILE],
                           wbuf[wslot][:, ic * 512 + oc4 * 128: ic * 512 + oc4 * 128 + 128],
                           ln1_fm[ic][:, :],
                           reads=[f"wbuf{wslot}", f"lnf{ic}"],
                           start=(ic == 0), stop=(ic == FC - 1), tag=f"prj{p}{oc}i{ic}")
                    def e(eng, p=p, oc=oc, bk=bk):
                        return eng.copy(pre[p][oc][:, 3:3 + TILE], PB[bk][:, 0:TILE])
                    sop(e, reads=[f"pb{bk}"], writes=[f"{p}_pre{oc}"], tag=f"prjE{p}{oc}")
                    gop(lambda e, p=p, oc=oc: e.tensor_copy(tails[p][:, oc * 3:oc * 3 + 3], pre[p][oc][:, TILE:TILE + 3]),
                        reads=[f"{p}_pre{oc}"], writes=[f"{p}_tail"], tag=f"ts{p}{oc}")
                    pcol = {"q": 0, "k": 1, "v": 2}[p]
                    def e0(eng, p=p, oc=oc, pcol=pcol):
                        return eng.tensor_scalar(cv[p][:, :], pre[p][oc][:, 0:TILE],
                                                 convw[:, pcol * 32 + oc * 4: pcol * 32 + oc * 4 + 1], None, ALU.mult)
                    vop(e0, reads=[f"{p}_pre{oc}", "convw_sb"], writes=[f"cv_{p}"], tag=f"cnv{p}{oc}0")
                    for i in (1, 2, 3):
                        def ei(eng, p=p, oc=oc, i=i, pcol=pcol):
                            return eng.scalar_tensor_tensor(cv[p][:, :], pre[p][oc][:, i:i + TILE],
                                                            convw[:, pcol * 32 + oc * 4 + i: pcol * 32 + oc * 4 + i + 1],
                                                            cv[p][:, :], ALU.mult, ALU.add)
                        vop(ei, reads=[f"{p}_pre{oc}", "convw_sb", f"cv_{p}"], writes=[f"cv_{p}"], tag=f"cnv{p}{oc}{i}")
                    def esilu(eng, p=p, oc=oc):
                        return eng.activation(fm[p][oc], cv[p][:, :], AF.Silu)
                    sop(esilu, reads=[f"cv_{p}"], writes=[f"{p}_pre{oc}"], tag=f"silu{p}{oc}")
                    if p == "q":
                        if oc == 6:
                            _emit_qsquare(0)
                        if oc == FC - 1:
                            _emit_qsquare(1)
                            hoist_queue.extend(_hoist_unit_list())
                    elif hoist_queue:
                        hoist_queue.pop(0)()
        for g in range(G):
            bk = rot("small")
            for ic in range(FC):
                mm(bk, PB[bk][:, 0:16], ln1_fm[ic][:, g * 128:(g + 1) * 128],
                   bmix_sb[:, ic * 16:(ic + 1) * 16],
                   reads=[f"lnf{ic}", "bmix_sb"], start=(ic == 0), stop=False, tag=f"bm{g}i{ic}")
            mm(bk, PB[bk][:, 0:16], consts["ones_row"][:, :], biasrow[0:1, :],
               reads=["ones_row_sb", "biasrow"], start=False, stop=True, tag=f"bmb{g}")
            def e(eng, g=g, bk=bk):
                return eng.activation(bm[g][:, :], PB[bk][:, 0:16], AF.Sigmoid)
            sop(e, reads=[f"pb{bk}"], writes=[f"bm{g}"], tag=f"bmS{g}")
            def e2(eng, g=g):
                return eng.activation(sqb[g][:, :], bm[g][:, 0:8], AF.Sqrt)
            sop(e2, reads=[f"bm{g}"], writes=[f"sqb{g}"], tag=f"sqb{g}")
        for g in range(G):
            if seq_start and g == 0:
                for h in range(H):
                    vop(lambda e, h=h: e.memset(S_sb[h][:, :], 0.0),
                        reads=[], writes=[f"S{h}"], tag=f"Sz{h}")
            for hp in range(0, H, 4):
                hs = list(range(hp, min(hp + 4, H)))
                gens = [attn_instance(t, g, h_) for h_ in hs]
                alive = [True] * len(gens)
                while any(alive):
                    for i_ in range(len(gens)):
                        if alive[i_]:
                            try:
                                next(gens[i_])
                            except StopIteration:
                                alive[i_] = False
            def post_chain(h):
                d3 = inst[h % 3]
                p3 = d3["_p"]
                gcol2 = bm[g][:, 8 + h:9 + h]
                def e(eng, h=h, gcol2=gcol2):
                    return eng.scalar_tensor_tensor(o_raw[h][:, :], o_raw[h][:, :], gcol2,
                                                    v_tok_t[h][:, :], ALU.mult, ALU.add)
                vop(e, reads=[f"o_raw{h}", f"bm{g}", f"v_tok_t{h}"], writes=[f"o_raw{h}"], tag=f"gmix{h}")
                sop(lambda e, h=h, d3=d3: e.activation(d3["sq_i"][:, :], o_raw[h][:, :], AF.Square, accum_out=d3["oss"][:, :]),
                    reads=[f"o_raw{h}"], writes=[f"oss_{p3}", f"sq_i_{p3}"], tag=f"osq{h}")
                vop(lambda e, d3=d3: e.tensor_scalar(d3["oss2"][:, :], d3["oss"][:, :], 1.0 / 128.0, 1e-5, ALU.mult, ALU.add),
                    reads=[f"oss_{p3}"], writes=[f"oss2_{p3}"], tag=f"oeps{h}")
                emit_rsqrt(d3["oss2"][:, :], f"oss2_{p3}", d3["rms_b"][:, :], f"rms_b_{p3}",
                           d3["nwa"][:, :], f"nwa_{p3}", d3["nwb"][:, :], f"nwb_{p3}", f"orsq{h}")
                vop(lambda e, h=h, d3=d3: e.tensor_scalar(o_raw[h][:, :], o_raw[h][:, :], d3["rms_b"][:, :], None, ALU.mult),
                    reads=[f"o_raw{h}", f"rms_b_{p3}"], writes=[f"o_raw{h}"], tag=f"onrm{h}")
            def post_pe(h):
                d3 = inst[h % 3]
                p3 = d3["_p"]
                bko = [0, 2, 1, 3][h % 4]
                tr(bko, PB[bko][:, 0:128], o_raw[h][:, :], reads=[f"o_raw{h}"], tag=f"oT{h}")
                sop(lambda e, bko=bko, d3=d3: e.copy(d3["ofm"][:, :], PB[bko][:, 0:128]),
                    reads=[f"pb{bko}"], writes=[f"ofm_{p3}"], tag=f"oTe{h}")
                for half in range(2):
                    ob = AOB[half]
                    mm(ob, PB[ob][:, 0:512], d3["ofm"][:, :], wo_sb[h][:, half * 512:(half + 1) * 512],
                       reads=[f"ofm_{p3}", f"wo{h}"], start=(h == 0), stop=(h == H - 1), tag=f"ao{h}_{half}")
            post_chain(0); post_chain(1); post_chain(2)
            for h in range(H):
                if h + 3 < H:
                    post_chain(h + 3)
                post_pe(h)
            for half in range(2):
                ob = AOB[half]
                def e(eng, g=g, half=half, ob=ob):
                    return eng.tensor_add(x_tok[g][:, half * 512:(half + 1) * 512],
                                          x_tok[g][:, half * 512:(half + 1) * 512],
                                          PB[ob][:, 0:512])
                vop(e, reads=[f"pb{ob}", f"x_tok{g}"], writes=[f"x_tok{g}"], tag=f"res1{g}{half}")
        for g in range(G):
            ln_token_major(g, t)
            for c in range(FC):
                bk = rot("tp")
                tr(bk, PB[bk][:, 0:128], lnt[g][:, c * 128:(c + 1) * 128],
                   reads=[f"lnt{g}"], tag=f"ln2T{t}{g}{c}")
                def e(eng, c=c, g=g, bk=bk):
                    return eng.copy(ln2_fm[c][:, g * 128:(g + 1) * 128], PB[bk][:, 0:128])
                sop(e, reads=[f"pb{bk}"], writes=[f"lnf{c}"], tag=f"ln2E{t}{g}{c}")
        for g in range(G):
            bk = rot("small")
            for ic in range(FC):
                mm(bk, PB[bk][:, 0:NE], ln2_fm[ic][:, g * 128:(g + 1) * 128],
                   sel_sb[:, ic * NE:(ic + 1) * NE],
                   reads=[f"lnf{ic}", "sel_sb"], start=(ic == 0), stop=(ic == FC - 1), tag=f"lg{g}{ic}")
            def e(eng, g=g, bk=bk):
                return eng.activation(sgate[g][:, :], PB[bk][:, 0:NE], AF.Sigmoid)
            sop(e, reads=[f"pb{bk}"], writes=[f"sgate{g}"], tag=f"sgS{g}")
            vop(lambda e, g=g: e.max(out=top8[:, :], in_=sgate[g][:, :]),
                reads=[f"sgate{g}"], writes=["top8"], tag=f"top8{g}")
            def e2(eng, g=g):
                return eng.tensor_scalar(gmask[:, :], sgate[g][:, :], top8[:, 3:4], None, ALU.is_ge)
            vop(e2, reads=[f"sgate{g}", "top8"], writes=["gmask"], tag=f"gmask{g}")
            def e3(eng, g=g):
                return eng.tensor_mul(sgate[g][:, :], sgate[g][:, :], gmask[:, :])
            vop(e3, reads=[f"sgate{g}", "gmask"], writes=[f"sgate{g}"], tag=f"gmul{g}")
            bk2 = rot("tp")
            tr(bk2, PB[bk2][0:64, 0:128], sgate[g][:, 0:NE], reads=[f"sgate{g}"], tag=f"gT{g}")
            def e4(eng, g=g, bk2=bk2):
                return eng.copy(g_fm[:, g * 128:(g + 1) * 128], PB[bk2][0:64, 0:128])
            sop(e4, reads=[f"pb{bk2}"], writes=["g_fm"], tag=f"gE{g}")
        dma(gdram, g_fm[:, :], "gdram", reads=["g_fm"], writes=["gdram"], tag=f"gW{t}")
        for e_i in range(NE):
            sl = e_i % 2
            if e_i % 2 == 0:
                gsl = (e_i // 2) % 2
                gsrc = AP(tensor=gdram.tensor, offset=e_i * TILE, ap=[[1, 1], [1, 2 * TILE]])
                dma(g8[gsl][0:1, :], gsrc, f"g8_{gsl}", reads=["gdram"],
                    writes=[f"g8_{gsl}"], tag=f"g8L{e_i}")
            ksrc = AP(tensor=keys_d.tensor, offset=e_i * D * 128,
                      ap=[[128, 128], [128 * 128, FC], [1, 128]])
            dma(ke_sb[sl][:, :], ksrc, f"ke{sl}", writes=[f"ke{sl}"], tag=f"keL{e_i}")
            dma(ve_sb[sl][:, :], values_d[e_i, :, :], f"ve{sl}", writes=[f"ve{sl}"], tag=f"veL{e_i}")
            hbk = [2, 3][e_i % 2]
            for ic in range(FC):
                mm(hbk, PB[hbk][:, 0:TILE], ke_sb[sl][:, ic * 128:(ic + 1) * 128],
                   ln2_fm[ic][:, :],
                   reads=[f"ke{sl}", f"lnf{ic}"], start=(ic == 0), stop=(ic == FC - 1),
                   tag=f"h{e_i}i{ic}")
            if e_i % 2 == 0:
                gbk = [6, 7][(e_i // 2) % 2]
                gsl = (e_i // 2) % 2
                gpr = (e_i // 2) % 2
                mm(gbk, PB[gbk][:, 0:2 * TILE], consts["ones_row"][:, :],
                   g8[gsl][0:1, 0:2 * TILE],
                   reads=["ones_row_sb", f"g8_{gsl}"], tag=f"gb{e_i}")
                def eg(eng, gpr=gpr, gbk=gbk):
                    return eng.copy(gb_sb[gpr][:, :], PB[gbk][:, 0:2 * TILE])
                sop(eg, reads=[f"pb{gbk}"], writes=[f"gb{gpr}"], tag=f"gbE{e_i}")
            gpr = (e_i // 2) % 2
            def es(eng, sl=sl, hbk=hbk, gpr=gpr, e_i=e_i):
                return eng.scalar_tensor_tensor(sc_sb[sl][:, :], PB[hbk][:, 0:TILE], 0.0,
                                                gb_sb[gpr][:, (e_i % 2) * TILE:(e_i % 2 + 1) * TILE],
                                                ALU.max, ALU.mult)
            vop(es, reads=[f"pb{hbk}", f"gb{gpr}"], writes=[f"sc{sl}"], tag=f"sc{e_i}")
            def emit_out2(e_j):
                slj = e_j % 2
                for g in range(G):
                    for half in range(2):
                        ob = OUT2B[(g, half)]
                        mm(ob, PB[ob][:, 0:512], sc_sb[slj][:, g * 128:(g + 1) * 128],
                           ve_sb[slj][:, half * 512:(half + 1) * 512],
                           reads=[f"sc{slj}", f"ve{slj}"], start=(e_j == 0), stop=(e_j == NE - 1),
                           tag=f"o2_{e_j}g{g}h{half}")
            if e_i >= 1:
                emit_out2(e_i - 1)
            if e_i == NE - 1:
                emit_out2(e_i)
        for g in range(G):
            for half in range(2):
                ob = OUT2B[(g, half)]
                def e(eng, g=g, half=half, ob=ob):
                    return eng.tensor_add(x_tok[g][:, half * 512:(half + 1) * 512],
                                          x_tok[g][:, half * 512:(half + 1) * 512],
                                          PB[ob][:, 0:512])
                vop(e, reads=[f"pb{ob}", f"x_tok{g}"], writes=[f"x_tok{g}"], tag=f"res2{g}{half}")
            r0 = t * TILE + g * 128
            dma(out_d[r0:r0 + 128, :], x_tok[g][:, :], "out",
                reads=[f"x_tok{g}"], tag=f"st{t}{g}")

    return nc, B, ctx




_CACHE = {}

def _get_program():
    if "nc" in _CACHE:
        return _CACHE["nc"]
    nc, B, ctx = build(NT=16)
    B.finalize(nc, None)
    sems = {}
    for e in ENGINES:
        sems[("eng", e)] = ctx.enter_context(nc.semaphore(f"sem_{e}"))
    for s in B.stream_totals:
        sems[("dma", s)] = ctx.enter_context(nc.semaphore(f"dma_{s}"))
    with nc.Block() as block:
        @block.sync
        def _(eng):
            B.emit_engine("sync", eng, sems)
            for s, c in B.stream_totals.items():
                eng.wait_ge(sems[("dma", s)], 16 * c)
        @block.tensor
        def _(eng):
            B.emit_engine("tensor", eng, sems)
        @block.vector
        def _(eng):
            B.emit_engine("vector", eng, sems)
        @block.scalar
        def _(eng):
            B.emit_engine("scalar", eng, sems)
        @block.gpsimd
        def _(eng):
            B.emit_engine("gpsimd", eng, sems)
    ctx.close()
    _CACHE["nc"] = nc
    return nc


def kernel(**inputs):
    from concourse.bass_utils import run_bass_kernel_spmd
    nc = _get_program()
    in_maps = [host_inputs(inputs, core) for core in range(8)]
    res = run_bass_kernel_spmd(nc, in_maps, core_ids=list(range(8)))
    out = np.concatenate([res.results[c]["out"].reshape(2, 2048, 1024) for c in range(8)], axis=0)
    return out


# version 14
# speedup vs baseline: 1.2986x; 1.0214x over previous
import numpy as np
import concourse.bass as bass
import concourse.mybir as mybir

F32 = mybir.dt.float32
AF = mybir.ActivationFunctionType
ALU = mybir.AluOpType

ENGINES = ("sync", "tensor", "vector", "scalar", "gpsimd")


class _Op:
    __slots__ = ("idx", "engine", "emit", "deps", "is_dma", "dma_stream",
                 "sig_idx", "waits", "tag", "drain_before")
    def __init__(self, idx, engine, emit, deps, is_dma, dma_stream, tag):
        self.idx = idx
        self.engine = engine
        self.emit = emit
        self.deps = deps
        self.is_dma = is_dma
        self.dma_stream = dma_stream
        self.sig_idx = None
        self.waits = None
        self.tag = tag
        self.drain_before = False


class Builder:
    def __init__(self):
        self.ops = []
        self.bufs = {}
        self.dma_streams = {}

    def _buf(self, name):
        if name not in self.bufs:
            self.bufs[name] = {"w": None, "r": []}
        return self.bufs[name]

    def op(self, engine, emit, reads=(), writes=(), tag="", dma_stream=None):
        assert engine in ENGINES, engine
        deps = set()
        for b in reads:
            e = self._buf(b)
            if e["w"] is not None:
                deps.add(e["w"])
        for b in writes:
            e = self._buf(b)
            if e["w"] is not None:
                deps.add(e["w"])
            deps.update(e["r"])
        o = _Op(len(self.ops), engine, emit, deps, dma_stream is not None, dma_stream, tag)
        self.ops.append(o)
        for b in reads:
            self._buf(b)["r"].append(o)
        for b in writes:
            self.bufs[b] = {"w": o, "r": []}
        return o

    def dma(self, out_ap, in_ap, stream, reads=(), writes=(), tag="", engine="sync"):
        def emit(eng):
            return eng.dma_start(out=out_ap, in_=in_ap)
        return self.op(engine, emit, reads=reads, writes=writes, tag=tag,
                       dma_stream=stream)

    def finalize(self, nc, block, final_engine="sync"):
        needs_sig = set()
        for o in self.ops:
            for d in o.deps:
                if d.is_dma or d.engine != o.engine:
                    needs_sig.add(d)
        for o in self.ops:
            if o.is_dma:
                needs_sig.add(o)
        eng_count = {e: 0 for e in ENGINES}
        stream_count = {}
        for o in self.ops:
            if o not in needs_sig:
                continue
            if o.is_dma:
                c = stream_count.get(o.dma_stream, 0) + 1
                stream_count[o.dma_stream] = c
                o.sig_idx = c * 16
            else:
                c = eng_count[o.engine] + 1
                eng_count[o.engine] = c
                o.sig_idx = c
        watermark = {}
        for o in self.ops:
            waits = {}
            for d in o.deps:
                if d.is_dma:
                    key = ("dma", d.dma_stream)
                elif d.engine != o.engine:
                    key = ("eng", d.engine)
                else:
                    continue
                assert d.sig_idx is not None
                if waits.get(key, 0) < d.sig_idx:
                    waits[key] = d.sig_idx
            lst = []
            for key, v in waits.items():
                wm = watermark.get((o.engine, key), 0)
                if v > wm:
                    watermark[(o.engine, key)] = v
                    lst.append((key, v))
            o.waits = lst
        last_drain = -1
        for o in self.ops:
            if o.engine != "vector" or o.is_dma:
                continue
            mx = -1
            for d in o.deps:
                if (not d.is_dma) and d.engine == "vector":
                    mx = max(mx, d.idx)
            if mx > last_drain:
                o.drain_before = True
                last_drain = o.idx
        self.stream_totals = stream_count
        return eng_count, stream_count

    def emit_engine(self, engname, engproxy, sems):
        for o in self.ops:
            if o.engine != engname:
                continue
            if o.drain_before:
                engproxy.drain()
            for key, v in o.waits:
                engproxy.wait_ge(sems[key], v)
            inst = o.emit(engproxy)
            if o.sig_idx is not None:
                if o.is_dma:
                    inst.then_inc(sems[("dma", o.dma_stream)], 16)
                else:
                    inst.then_inc(sems[("eng", engname)], 1)




"""Megakernel program builder v2: 2-way pipelined attention instances,
overlapped weight streaming. Residual token-major; delta rule chunk=128 via
exact (I-B)^-1 = prod(I+B^{2^k}); all matmuls fp32."""
import numpy as np
from contextlib import ExitStack
import concourse.bass as bass
import concourse.mybir as mybir
from concourse.bass_types import AP

F32 = mybir.dt.float32
AF = mybir.ActivationFunctionType
ALU = mybir.AluOpType
AX = mybir.AxisListType

TILE = 256
G = 2
H = 8
FC = 8
D = 1024
NE = 64
NLEV = 7


def host_constants():
    i_idx = np.arange(128)[:, None]
    j_idx = np.arange(128)[None, :]
    return {
        "ident": np.eye(128, dtype=np.float32),
        "m_low_neg": np.where(i_idx > j_idx, -1.0, 0.0).astype(np.float32),
        "m_up_neg": np.where(i_idx < j_idx, -1.0, 0.0).astype(np.float32),
        "m_upincl": np.where(i_idx <= j_idx, 1.0, 0.0).astype(np.float32),
        "ones_col": np.ones((128, 1), np.float32),
        "ones_row": np.ones((1, 128), np.float32),
    }


def host_inputs(full, core):
    x = np.asarray(full["L0_attn_hidden_states"])
    xs = np.ascontiguousarray(x[2 * core:2 * core + 2].reshape(4096, D))
    conv = np.stack([np.asarray(full["L0_attn_conv_q_w"]),
                     np.asarray(full["L0_attn_conv_k_w"]),
                     np.asarray(full["L0_attn_conv_v_w"])])
    conv = conv.reshape(3, FC, 128, 4).transpose(2, 0, 1, 3).reshape(128, 96)
    bmix = np.concatenate([np.asarray(full["L0_attn_b_w"]),
                           np.asarray(full["L0_attn_mix_w"])], axis=1)
    m = {
        "x_in": xs,
        "wq": np.asarray(full["L0_attn_q_w"]),
        "wk": np.asarray(full["L0_attn_k_w"]),
        "wv": np.asarray(full["L0_attn_v_w"]),
        "wo": np.asarray(full["L0_attn_o_w"]),
        "convw": np.ascontiguousarray(conv),
        "bmix_w": np.ascontiguousarray(bmix),
        "mix_b": np.asarray(full["L0_attn_mix_b"]).reshape(1, 8),
        "mix_bias": np.asarray(full["L0_attn_mix_bias"]).reshape(1, 8),
        "onorm": np.asarray(full["L0_attn_o_norm_w"]).reshape(128, 1),
        "sel_r": np.ascontiguousarray(np.asarray(full["L0_ffn_expert_sel"]).T),
        "keys": np.asarray(full["L0_ffn_keys"]),
        "values": np.asarray(full["L0_ffn_values"]),
    }
    m.update(host_constants())
    return {k: np.ascontiguousarray(v, dtype=np.float32) for k, v in m.items()}


def build(NT=16, tiles_per_seq=8):
    nc = bass.Bass("TRN2", target_bir_lowering=False)
    B = Builder()
    ctx = ExitStack()

    dram_in = {}
    def din(name, shape):
        dram_in[name] = nc.dram_tensor(name, shape, F32, kind="ExternalInput").ap()
        return dram_in[name]

    x_in = din("x_in", [4096, D])
    wq = din("wq", [D, D]); wk = din("wk", [D, D]); wv = din("wv", [D, D])
    wo = din("wo", [D, D])
    convw_d = din("convw", [128, 96])
    bmix_d = din("bmix_w", [D, 16])
    mixb_d = din("mix_b", [1, 8]); mixbias_d = din("mix_bias", [1, 8])
    onorm_d = din("onorm", [128, 1])
    sel_d = din("sel_r", [D, NE])
    keys_d = din("keys", [NE, D, 128])
    values_d = din("values", [NE, 128, D])
    for cn in ("ident", "m_low_neg", "m_up_neg", "m_upincl"):
        din(cn, [128, 128])
    din("ones_col", [128, 1]); din("ones_row", [1, 128])

    out_d = nc.dram_tensor("out", [4096, D], F32, kind="ExternalOutput").ap()
    gdram = nc.dram_tensor("g_dram", [NE, TILE], F32, kind="Internal").ap()

    sbufs = {}
    def sb(name, shape):
        h = ctx.enter_context(nc.sbuf_tensor(name, shape, F32))
        sbufs[name] = h
        return h

    wbuf = [sb(f"wbuf{i}", [128, FC * 512]) for i in range(2)]
    wo_sb = [sb(f"wo{h}", [128, D]) for h in range(H)]
    sel_sb = sb("sel_sb", [128, FC * NE])
    ke_sb = [sb(f"ke{i}", [128, D]) for i in range(2)]
    ve_sb = [sb(f"ve{i}", [128, D]) for i in range(2)]
    x_tok = [sb(f"x_tok{g}", [128, D]) for g in range(G)]
    lnt = [sb(f"lnt{g}", [128, D]) for g in range(G)]
    ln1_fm = [sb(f"lnf{c}", [128, TILE]) for c in range(FC)]
    ln2_fm = ln1_fm
    pre = {p: [sb(f"{p}_pre{c}", [128, TILE + 3]) for c in range(FC)] for p in "qkv"}
    fm = {p: [pre[p][c][:, 3:3 + TILE] for c in range(FC)] for p in "qkv"}
    tails = {p: sb(f"{p}_tail", [128, 3 * FC]) for p in "qkv"}
    cv = {p: sb(f"cv_{p}", [128, TILE]) for p in "qkv"}
    convw = sb("convw_sb", [128, 96])
    bmix_sb = sb("bmix_sb", [128, FC * 16])
    biasrow = sb("biasrow", [1, 16])
    tmp8 = sb("tmp8", [1, 8])
    onorm_sb = sb("onorm_sb", [128, 1])
    bm = [sb(f"bm{g}", [128, 16]) for g in range(G)]
    sqb = [sb(f"sqb{g}", [128, 8]) for g in range(G)]
    S_sb = [sb(f"S{h}", [128, 128]) for h in range(H)]
    consts = {cn: sb(cn + "_sb", list(np.shape(host_constants()[cn])))
              for cn in ("ident", "m_low_neg", "m_up_neg", "m_upincl",
                          "ones_col", "ones_row")}
    inst = []
    for pn in range(4):
        dd = {}
        names = [("k_tok", [128, 128]), ("uw0", [128, 256]),
                 ("ktf", [128, 128]),
                 ("attnT", [128, 128]), ("u_p", [128, 128]),
                 ("Ma", [128, 128]), ("sq_i", [128, 128]),
                 ("kss", [128, 1]), ("kss2", [128, 1]), ("rsqk", [128, 1]),
                 ("nwa", [128, 1]), ("nwb", [128, 1])]
        if pn < 3:
            names += [("ofm", [128, 128]), ("oss", [128, 1]), ("oss2", [128, 1]),
                      ("rms_b", [128, 1])]
        for nm, shp in names:
            dd[nm] = sb(f"{nm}_{pn}", shp)
        dd["P"] = [sb(f"P{k}_{pn}", [128, 128]) for k in range(2)]
        dd["PT"] = [sb(f"PT{k}_{pn}", [128, 128]) for k in range(2)]
        dd["_p"] = pn
        inst.append(dd)
    nw1 = sb("nw1", [128, 1]); nw2 = sb("nw2", [128, 1])
    qhf_t = [sb(f"qhf_t{h}", [128, TILE]) for h in range(H)]
    v_tok_t = [sb(f"v_tok_t{h}", [128, 128]) for h in range(H)]
    o_raw = [sb(f"o_raw{h}", [128, 128]) for h in range(H)]
    qsq_t = [sb(f"qsq_t{h}", [128, TILE]) for h in range(2)]
    qrow_t = [sb(f"qrow_t{h}", [1, TILE]) for h in range(2)]
    rsqq_t = [sb(f"rsqq_t{h}", [1, TILE]) for h in range(2)]
    nra_t = [sb(f"nra_t{h}", [1, TILE]) for h in range(2)]
    ml = [sb(f"ml{g}", [128, 1]) for g in range(G)]
    rl = [sb(f"rl{g}", [128, 1]) for g in range(G)]
    ssl = sb("ssl", [128, 1]); v1l = sb("v1l", [128, 1]); m2l = sb("m2l", [128, 1])
    sgate = [sb(f"sgate{g}", [128, NE]) for g in range(G)]
    top8 = sb("top8", [128, 8]); gmask = sb("gmask", [128, NE])
    g_fm = sb("g_fm", [64, TILE])
    sc_sb = [sb(f"sc{i}", [128, TILE]) for i in range(2)]
    gb_sb = [sb(f"gb{i}", [128, 2 * TILE]) for i in range(2)]
    g8 = [sb(f"g8_{i}", [1, 2 * TILE]) for i in range(2)]

    psum = [ctx.enter_context(nc.psum_tensor(f"pb{i}", [128, 512], F32)) for i in range(8)]
    PB = [p.ap() for p in psum]

    AOB = [4, 5]
    OUT2B = {(0, 0): 4, (0, 1): 5, (1, 0): 0, (1, 1): 1}
    _rot = {"tp": 0, "mm": 0, "small": 0}
    def rot(kind):
        lst = {"tp": [0, 1], "mm": [2, 3], "small": [6, 7]}[kind]
        i = _rot[kind] % len(lst); _rot[kind] += 1
        return lst[i]

    def dma(out_ap, in_ap, stream, reads=(), writes=(), tag=""):
        return B.dma(out_ap, in_ap, stream, reads=reads, writes=writes, tag=tag)

    def mm(bank, out_ap, lhsT, rhs, reads, start=True, stop=True, tag=""):
        def e(eng, out_ap=out_ap, lhsT=lhsT, rhs=rhs, start=start, stop=stop):
            return eng.matmul(out_ap, lhsT, rhs, start=start, stop=stop)
        return B.op("tensor", e, reads=list(reads), writes=[f"pb{bank}"], tag=tag)

    def tr(bank, out_ap, in_ap, reads, tag=""):
        def e(eng, out_ap=out_ap, in_ap=in_ap):
            return eng.transpose(out_ap, in_ap, consts["ident"][:, :])
        return B.op("tensor", e, reads=list(reads) + ["ident_sb"], writes=[f"pb{bank}"], tag=tag)

    def vop(emit, reads, writes, tag=""):
        return B.op("vector", emit, reads=reads, writes=writes, tag=tag)
    def sop(emit, reads, writes, tag=""):
        return B.op("scalar", emit, reads=reads, writes=writes, tag=tag)
    def gop(emit, reads, writes, tag=""):
        return B.op("gpsimd", emit, reads=reads, writes=writes, tag=tag)

    def emit_rsqrt(v_ap, vname, out_ap, outname, t1_ap, t1n, t2_ap, t2n, tag):
        vop(lambda e: e.reciprocal(t1_ap, v_ap), reads=[vname], writes=[t1n], tag=tag + "_rc")
        sop(lambda e: e.activation(t2_ap, t1_ap, AF.Sqrt), reads=[t1n], writes=[t2n], tag=tag + "_sq")
        gop(lambda e: e.tensor_mul(t1_ap, t2_ap, t2_ap), reads=[t2n], writes=[t1n], tag=tag + "_n1")
        gop(lambda e: e.tensor_mul(t1_ap, t1_ap, v_ap), reads=[t1n, vname], writes=[t1n], tag=tag + "_n2")
        gop(lambda e: e.tensor_scalar(t1_ap, t1_ap, -0.5, 1.5, ALU.mult, ALU.add), reads=[t1n], writes=[t1n], tag=tag + "_n3")
        gop(lambda e: e.tensor_mul(out_ap, t2_ap, t1_ap), reads=[t1n, t2n], writes=[outname], tag=tag + "_n4")

    def ln_token_major(g, t):
        xn, lname = f"x_tok{g}", f"lnt{g}"
        mname, rname = f"m{g}", f"r{g}"
        x_ap = x_tok[g][:, :]; ln_ap = lnt[g][:, :]
        m_t, r_t = ml[g], rl[g]
        vop(lambda e: e.reduce_sum(m_t[:, :], x_ap, axis=AX.X), reads=[xn], writes=[mname], tag=f"lnsum{t}{g}")
        vop(lambda e: e.tensor_scalar(m_t[:, :], m_t[:, :], 1.0 / 1024.0, None, ALU.mult),
            reads=[mname], writes=[mname], tag=f"lnm{t}{g}")
        sop(lambda e: e.activation(ln_ap, x_ap, AF.Square, accum_out=ssl[:, :]),
            reads=[xn], writes=["ssl", lname], tag=f"lnsq{t}{g}")
        vop(lambda e: e.tensor_scalar(v1l[:, :], ssl[:, :], 1.0 / 1024.0, 1e-5, ALU.mult, ALU.add),
            reads=["ssl"], writes=["v1l"], tag=f"lnv1{t}{g}")
        vop(lambda e: e.tensor_mul(m2l[:, :], m_t[:, :], m_t[:, :]), reads=[mname], writes=["m2l"], tag=f"lnm2{t}{g}")
        vop(lambda e: e.tensor_sub(v1l[:, :], v1l[:, :], m2l[:, :]), reads=["v1l", "m2l"], writes=["v1l"], tag=f"lnv{t}{g}")
        emit_rsqrt(v1l[:, :], "v1l", r_t[:, :], rname, nw1[:, :], "nw1", nw2[:, :], "nw2", f"lnr{t}{g}")
        vop(lambda e: e.tensor_scalar(ln_ap, x_ap, m_t[:, :], r_t[:, :], ALU.subtract, ALU.mult),
            reads=[xn, mname, rname], writes=[lname], tag=f"lnn{t}{g}")

    def attn_instance(t, g, h):
        d = inst[h % 4]
        pn = d["_p"]
        def N(nm):
            return f"{nm}_{pn}"
        mmb = {0: [0, 2], 1: [1, 3], 2: [6, 7], 3: [4, 5]}[pn]
        mi = [0]
        def mrot():
            b = mmb[mi[0] % 2]; mi[0] += 1
            return b
        qs = fm["q"][h][:, g * 128:(g + 1) * 128]
        ks = fm["k"][h][:, g * 128:(g + 1) * 128]
        vs = fm["v"][h][:, g * 128:(g + 1) * 128]
        kn, qn, vn = f"k_pre{h}", f"q_pre{h}", f"v_pre{h}"
        k_tok = d["k_tok"]; uw0 = d["uw0"]; uw_s = d["uw0"]
        ktf = d["ktf"]; wtf = d["ktf"]; attnT = d["attnT"]; u_p = d["u_p"]
        qhf = qhf_t[h][:, g * 128:(g + 1) * 128]
        Ma = d["Ma"]
        P = d["P"]; PT = d["PT"]; sq_i = d["sq_i"]
        sqb_col = sqb[g][:, h:h + 1]
        gcol = bm[g][:, 8 + h:9 + h]
        Sn = f"S{h}"

        b_ = mrot()
        tr(b_, PB[b_][:, 0:128], ks, reads=[kn], tag=f"kT{t}{g}{h}")
        sop(lambda e, b_=b_: e.copy(k_tok[:, :], PB[b_][:, 0:128]), reads=[f"pb{b_}"], writes=[N("k_tok")], tag="kTe")
        sop(lambda e: e.activation(sq_i[:, :], k_tok[:, :], AF.Square, accum_out=d["kss"][:, :]),
            reads=[N("k_tok")], writes=[N("kss"), N("sq_i")], tag="ksq")
        vop(lambda e: e.tensor_scalar(d["kss2"][:, :], d["kss"][:, :], 1e-12, None, ALU.add),
            reads=[N("kss")], writes=[N("kss2")], tag="keps")
        emit_rsqrt(d["kss2"][:, :], N("kss2"), d["rsqk"][:, :], N("rsqk"),
                   d["nwa"][:, :], N("nwa"), d["nwb"][:, :], N("nwb"), f"krsq{pn}")
        yield
        b_ = mrot()
        tr(b_, PB[b_][:, 0:128], vs, reads=[vn], tag=f"vT{t}{g}{h}")
        sop(lambda e, b_=b_: e.copy(v_tok_t[h][:, :], PB[b_][:, 0:128]), reads=[f"pb{b_}"], writes=[f"v_tok_t{h}"], tag="vTe")
        vop(lambda e: e.tensor_scalar(k_tok[:, :], k_tok[:, :], d["rsqk"][:, :], None, ALU.mult),
            reads=[N("k_tok"), N("rsqk")], writes=[N("k_tok")], tag="khat")
        vop(lambda e: e.tensor_scalar(uw0[:, 128:256], k_tok[:, :], sqb_col, None, ALU.mult),
            reads=[N("k_tok"), f"sqb{g}"], writes=[N("uw0")], tag="ktil")
        vop(lambda e: e.tensor_scalar(uw0[:, 0:128], v_tok_t[h][:, :], sqb_col, None, ALU.mult),
            reads=[f"v_tok_t{h}", f"sqb{g}"], writes=[N("uw0")], tag="vtil")
        yield
        b_ = mrot()
        tr(b_, PB[b_][:, 0:128], uw0[:, 128:256], reads=[N("uw0")], tag="ktfT")
        sop(lambda e, b_=b_: e.copy(ktf[:, :], PB[b_][:, 0:128]), reads=[f"pb{b_}"], writes=[N("ktf")], tag="ktfE")
        yield
        b0 = mrot()
        mm(b0, PB[b0][:, 0:128], ktf[:, :], ktf[:, :], reads=[N("ktf")], tag="Gt")
        def e(eng, b0=b0):
            return eng.tensor_tensor(P[0][:, :], PB[b0][:, 0:128], consts["m_low_neg"][:, :], ALU.mult)
        vop(e, reads=[f"pb{b0}", "m_low_neg_sb"], writes=[N("P0")], tag="Bev")
        def e(eng, b0=b0):
            return eng.tensor_tensor(PT[0][:, :], PB[b0][:, 0:128], consts["m_up_neg"][:, :], ALU.mult)
        vop(e, reads=[f"pb{b0}", "m_up_neg_sb"], writes=[N("PT0")], tag="BTev")
        vop(lambda e: e.tensor_add(Ma[:, :], PT[0][:, :], consts["ident"][:, :]),
            reads=[N("PT0"), "ident_sb"], writes=[N("Ma")], tag="M1")
        yield
        Man = N("Ma")
        for k in range(1, NLEV):
            sl_ = k % 2; slp = (k - 1) % 2
            bk = mrot()
            mm(bk, PB[bk][:, 0:128], PT[slp][:, :], P[slp][:, :],
               reads=[N(f"PT{slp}"), N(f"P{slp}")], tag=f"sqP{k}")
            ep = sop if k % 2 == 0 else vop
            def e(eng, sl_=sl_, bk=bk):
                if hasattr(eng, "copy"):
                    return eng.copy(P[sl_][:, :], PB[bk][:, 0:128])
                return eng.tensor_copy(P[sl_][:, :], PB[bk][:, 0:128])
            ep(e, reads=[f"pb{bk}"], writes=[N(f"P{sl_}")], tag=f"sqPe{k}")
            if k < NLEV - 1:
                bk2 = mrot()
                mm(bk2, PB[bk2][:, 0:128], P[slp][:, :], PT[slp][:, :],
                   reads=[N(f"P{slp}"), N(f"PT{slp}")], tag=f"sqT{k}")
                ep2 = vop if k % 2 == 0 else sop
                def e(eng, sl_=sl_, bk2=bk2):
                    if hasattr(eng, "copy"):
                        return eng.copy(PT[sl_][:, :], PB[bk2][:, 0:128])
                    return eng.tensor_copy(PT[sl_][:, :], PB[bk2][:, 0:128])
                ep2(e, reads=[f"pb{bk2}"], writes=[N(f"PT{sl_}")], tag=f"sqTe{k}")
            yield
            bk3 = mrot()
            mm(bk3, PB[bk3][:, 0:128], P[sl_][:, :], Ma[:, :],
               reads=[N(f"P{sl_}"), Man], tag=f"ap{k}")
            def e(eng, bk3=bk3):
                return eng.tensor_add(Ma[:, :], Ma[:, :], PB[bk3][:, 0:128])
            vop(e, reads=[f"pb{bk3}", Man], writes=[Man], tag=f"apE{k}")
            yield
        bk = mrot()
        mm(bk, PB[bk][:, 0:256], Ma[:, :], uw0[:, :], reads=[Man, N("uw0")], tag="uwmm")
        def e(eng, bk=bk):
            return eng.tensor_scalar(uw_s[:, :], PB[bk][:, 0:256], sqb_col, None, ALU.mult)
        vop(e, reads=[f"pb{bk}", f"sqb{g}"], writes=[N("uw0")], tag="uwE")
        yield
        b_ = mrot()
        tr(b_, PB[b_][:, 0:128], uw_s[:, 128:256], reads=[N("uw0")], tag="wT")
        sop(lambda e, b_=b_: e.copy(wtf[:, :], PB[b_][:, 0:128]), reads=[f"pb{b_}"], writes=[N("ktf")], tag="wTe")
        yield
        bk = mrot()
        mm(bk, PB[bk][:, 0:128], wtf[:, :], S_sb[h][:, :], reads=[N("ktf"), Sn], tag="wS")
        def e(eng, bk=bk):
            return eng.tensor_sub(u_p[:, :], uw_s[:, 0:128], PB[bk][:, 0:128])
        vop(e, reads=[f"pb{bk}", N("uw0")], writes=[N("u_p")], tag="uprime")
        bk = mrot()
        mm(bk, PB[bk][:, 0:128], ks, qhf, reads=[kn, f"qhf_t{h}"], tag="aT")
        def e(eng, bk=bk):
            return eng.scalar_tensor_tensor(attnT[:, :], PB[bk][:, 0:128], d["rsqk"][:, :],
                                            consts["m_upincl"][:, :], ALU.mult, ALU.mult)
        vop(e, reads=[f"pb{bk}", N("rsqk"), "m_upincl_sb"], writes=[N("attnT")], tag="aTe")
        yield
        bk = mrot()
        mm(bk, PB[bk][:, 0:128], qhf, S_sb[h][:, :], reads=[f"qhf_t{h}", Sn], start=True, stop=False, tag="qS")
        mm(bk, PB[bk][:, 0:128], attnT[:, :], u_p[:, :], reads=[N("attnT"), N("u_p")], start=False, stop=True, tag="au")
        def e(eng, bk=bk):
            return eng.tensor_sub(o_raw[h][:, :], PB[bk][:, 0:128], v_tok_t[h][:, :])
        vop(e, reads=[f"pb{bk}", f"v_tok_t{h}"], writes=[f"o_raw{h}"], tag="gsub")
        bk = mrot()
        mm(bk, PB[bk][:, 0:128], k_tok[:, :], u_p[:, :], reads=[N("k_tok"), N("u_p")], tag="Sup")
        def e(eng, bk=bk):
            return eng.tensor_add(S_sb[h][:, :], S_sb[h][:, :], PB[bk][:, 0:128])
        vop(e, reads=[f"pb{bk}", Sn], writes=[Sn], tag="SupE")

    dma(convw[:, :], convw_d, "convw", writes=["convw_sb"])
    dma(biasrow[0:1, 8:16], mixb_d, "biasrow", writes=["biasrow"])
    dma(tmp8[:, :], mixbias_d, "tmp8", writes=["tmp8"])
    dma(onorm_sb[:, :], onorm_d, "onorm", writes=["onorm_sb"])
    for cn in consts:
        nm = cn + "_sb"
        dma(sbufs[nm][:, :], dram_in[cn], nm, writes=[nm])
    sel_src = AP(tensor=sel_d.tensor, offset=0, ap=[[NE, 128], [128 * NE, FC], [1, NE]])
    dma(sel_sb[:, :], sel_src, "sel_sb", writes=["sel_sb"])
    for h in range(H):
        dma(wo_sb[h][:, :], wo[h * 128:(h + 1) * 128, :], f"wo{h}", writes=[f"wo{h}"])
    gop(lambda e: e.memset(biasrow[0:1, 0:8], 0.0), reads=["biasrow"], writes=["biasrow"], tag="biasz")
    vop(lambda e: e.tensor_add(biasrow[0:1, 8:16], biasrow[0:1, 8:16], tmp8[0:1, :]),
        reads=["biasrow", "tmp8"], writes=["biasrow"], tag="biasadd")
    for h in range(H):
        def e(eng, h=h):
            return eng.tensor_scalar(wo_sb[h][:, :], wo_sb[h][:, :], onorm_sb[:, :], None, ALU.mult)
        vop(e, reads=[f"wo{h}", "onorm_sb"], writes=[f"wo{h}"], tag="wo_scale")
    bmix_src = AP(tensor=bmix_d.tensor, offset=0, ap=[[16, 128], [128 * 16, FC], [1, 16]])
    dma(bmix_sb[:, :], bmix_src, "bmix_sb", writes=["bmix_sb"])

    for t in range(NT):
        seq_start = (t % tiles_per_seq) == 0
        for g in range(G):
            r0 = t * TILE + g * 128
            dma(x_tok[g][:, :], x_in[r0:r0 + 128, :], f"x_tok{g}",
                writes=[f"x_tok{g}"], tag=f"xin{t}g{g}")
        for g in range(G):
            ln_token_major(g, t)
            for c in range(FC):
                bk = rot("tp")
                tr(bk, PB[bk][:, 0:128], lnt[g][:, c * 128:(c + 1) * 128],
                   reads=[f"lnt{g}"], tag=f"ln1T{t}{g}{c}")
                def e(eng, c=c, g=g, bk=bk):
                    return eng.copy(ln1_fm[c][:, g * 128:(g + 1) * 128], PB[bk][:, 0:128])
                sop(e, reads=[f"pb{bk}"], writes=[f"lnf{c}"], tag=f"ln1E{t}{g}{c}")
        hoist_queue = []
        def _hoist_unit_list():
            units = []
            for h_ in range(H):
                units.append(lambda h_=h_: (_emit_qss(h_), (_emit_qsquare(h_ + 2) if h_ + 2 < H else None)))
                if h_ >= 1:
                    units.append(lambda h_=h_: _emit_qbc(h_ - 1))
            units.append(lambda: _emit_qbc(H - 1))
            return units
        def _emit_qbc(h):
            hp2 = h % 2
            qfm_h = fm["q"][h]
            qnh = f"q_pre{h}"
            bk2 = rot("small")
            mm(bk2, PB[bk2][:, 0:TILE], consts["ones_row"][:, :], rsqq_t[hp2][0:1, :],
               reads=["ones_row_sb", f"rsqq_t{hp2}"], tag=f"qbcH{h}")
            def e(eng, bk2=bk2, qfm_h=qfm_h, h=h):
                return eng.tensor_tensor(qhf_t[h][:, :], qfm_h, PB[bk2][:, 0:TILE], ALU.mult)
            vop(e, reads=[qnh, f"pb{bk2}"], writes=[f"qhf_t{h}"], tag=f"qhatH{h}")
        def _emit_qsquare(h):
            hp2 = h % 2
            qfm_h = fm["q"][h]
            qnh = f"q_pre{h}"
            sop(lambda e, qfm_h=qfm_h, hp2=hp2: e.activation(qsq_t[hp2][:, :], qfm_h, AF.Square),
                reads=[qnh], writes=[f"qsq_t{hp2}"], tag=f"qsqH{h}")
        def _emit_qss(h):
            hp2 = h % 2
            qfm_h = fm["q"][h]
            qnh = f"q_pre{h}"
            bk = rot("small")
            mm(bk, PB[bk][0:1, 0:TILE], consts["ones_col"][:, :], qsq_t[hp2][:, :],
               reads=["ones_col_sb", f"qsq_t{hp2}"], tag=f"qssH{h}")
            def e(eng, bk=bk, hp2=hp2):
                return eng.tensor_scalar(qrow_t[hp2][0:1, :], PB[bk][0:1, 0:TILE], 1e-12, None, ALU.add)
            vop(e, reads=[f"pb{bk}"], writes=[f"qrow_t{hp2}"], tag=f"qepsH{h}")
            emit_rsqrt(qrow_t[hp2][0:1, :], f"qrow_t{hp2}", rsqq_t[hp2][0:1, :], f"rsqq_t{hp2}",
                       nra_t[hp2][0:1, :], f"nra_t{hp2}", rsqq_t[hp2][0:1, :], f"rsqq_t{hp2}", f"qrsqH{h}")
        for pi, (p, wd) in enumerate((("q", wq), ("k", wk), ("v", wv))):
            for c in range(FC):
                if seq_start:
                    gop(lambda e, p=p, c=c: e.memset(pre[p][c][:, 0:3], 0.0),
                        reads=[], writes=[f"{p}_pre{c}"], tag=f"tz{p}{c}")
                else:
                    gop(lambda e, p=p, c=c: e.tensor_copy(pre[p][c][:, 0:3], tails[p][:, c * 3:c * 3 + 3]),
                        reads=[f"{p}_tail"], writes=[f"{p}_pre{c}"], tag=f"tc{p}{c}")
            for hf in range(2):
                wslot = (pi * 2 + hf) % 2
                wsrc = AP(tensor=wd.tensor, offset=hf * 512,
                          ap=[[D, 128], [128 * D, FC], [1, 512]])
                dma(wbuf[wslot][:, :], wsrc, f"wbuf{wslot}", writes=[f"wbuf{wslot}"], tag=f"w{p}{t}h{hf}")
                for oc4 in range(4):
                    oc = hf * 4 + oc4
                    bk = rot("mm")
                    for ic in range(FC):
                        mm(bk, PB[bk][:, 0:TILE],
                           wbuf[wslot][:, ic * 512 + oc4 * 128: ic * 512 + oc4 * 128 + 128],
                           ln1_fm[ic][:, :],
                           reads=[f"wbuf{wslot}", f"lnf{ic}"],
                           start=(ic == 0), stop=(ic == FC - 1), tag=f"prj{p}{oc}i{ic}")
                    def e(eng, p=p, oc=oc, bk=bk):
                        return eng.copy(pre[p][oc][:, 3:3 + TILE], PB[bk][:, 0:TILE])
                    sop(e, reads=[f"pb{bk}"], writes=[f"{p}_pre{oc}"], tag=f"prjE{p}{oc}")
                    gop(lambda e, p=p, oc=oc: e.tensor_copy(tails[p][:, oc * 3:oc * 3 + 3], pre[p][oc][:, TILE:TILE + 3]),
                        reads=[f"{p}_pre{oc}"], writes=[f"{p}_tail"], tag=f"ts{p}{oc}")
                    pcol = {"q": 0, "k": 1, "v": 2}[p]
                    def e0(eng, p=p, oc=oc, pcol=pcol):
                        return eng.tensor_scalar(cv[p][:, :], pre[p][oc][:, 0:TILE],
                                                 convw[:, pcol * 32 + oc * 4: pcol * 32 + oc * 4 + 1], None, ALU.mult)
                    vop(e0, reads=[f"{p}_pre{oc}", "convw_sb"], writes=[f"cv_{p}"], tag=f"cnv{p}{oc}0")
                    for i in (1, 2, 3):
                        def ei(eng, p=p, oc=oc, i=i, pcol=pcol):
                            return eng.scalar_tensor_tensor(cv[p][:, :], pre[p][oc][:, i:i + TILE],
                                                            convw[:, pcol * 32 + oc * 4 + i: pcol * 32 + oc * 4 + i + 1],
                                                            cv[p][:, :], ALU.mult, ALU.add)
                        vop(ei, reads=[f"{p}_pre{oc}", "convw_sb", f"cv_{p}"], writes=[f"cv_{p}"], tag=f"cnv{p}{oc}{i}")
                    def esilu(eng, p=p, oc=oc):
                        return eng.activation(fm[p][oc], cv[p][:, :], AF.Silu)
                    sop(esilu, reads=[f"cv_{p}"], writes=[f"{p}_pre{oc}"], tag=f"silu{p}{oc}")
                    if p == "q":
                        if oc == 6:
                            _emit_qsquare(0)
                        if oc == FC - 1:
                            _emit_qsquare(1)
                            hoist_queue.extend(_hoist_unit_list())
                    elif hoist_queue:
                        hoist_queue.pop(0)()
        for g in range(G):
            bk = rot("small")
            for ic in range(FC):
                mm(bk, PB[bk][:, 0:16], ln1_fm[ic][:, g * 128:(g + 1) * 128],
                   bmix_sb[:, ic * 16:(ic + 1) * 16],
                   reads=[f"lnf{ic}", "bmix_sb"], start=(ic == 0), stop=False, tag=f"bm{g}i{ic}")
            mm(bk, PB[bk][:, 0:16], consts["ones_row"][:, :], biasrow[0:1, :],
               reads=["ones_row_sb", "biasrow"], start=False, stop=True, tag=f"bmb{g}")
            def e(eng, g=g, bk=bk):
                return eng.activation(bm[g][:, :], PB[bk][:, 0:16], AF.Sigmoid)
            sop(e, reads=[f"pb{bk}"], writes=[f"bm{g}"], tag=f"bmS{g}")
            def e2(eng, g=g):
                return eng.activation(sqb[g][:, :], bm[g][:, 0:8], AF.Sqrt)
            sop(e2, reads=[f"bm{g}"], writes=[f"sqb{g}"], tag=f"sqb{g}")
        for g in range(G):
            if seq_start and g == 0:
                for h in range(H):
                    vop(lambda e, h=h: e.memset(S_sb[h][:, :], 0.0),
                        reads=[], writes=[f"S{h}"], tag=f"Sz{h}")
            for hp in range(0, H, 4):
                hs = list(range(hp, min(hp + 4, H)))
                gens = [attn_instance(t, g, h_) for h_ in hs]
                alive = [True] * len(gens)
                while any(alive):
                    for i_ in range(len(gens)):
                        if alive[i_]:
                            try:
                                next(gens[i_])
                            except StopIteration:
                                alive[i_] = False
            def post_chain(h):
                d3 = inst[h % 3]
                p3 = d3["_p"]
                gcol2 = bm[g][:, 8 + h:9 + h]
                def e(eng, h=h, gcol2=gcol2):
                    return eng.scalar_tensor_tensor(o_raw[h][:, :], o_raw[h][:, :], gcol2,
                                                    v_tok_t[h][:, :], ALU.mult, ALU.add)
                vop(e, reads=[f"o_raw{h}", f"bm{g}", f"v_tok_t{h}"], writes=[f"o_raw{h}"], tag=f"gmix{h}")
                sop(lambda e, h=h, d3=d3: e.activation(d3["sq_i"][:, :], o_raw[h][:, :], AF.Square, accum_out=d3["oss"][:, :]),
                    reads=[f"o_raw{h}"], writes=[f"oss_{p3}", f"sq_i_{p3}"], tag=f"osq{h}")
                vop(lambda e, d3=d3: e.tensor_scalar(d3["oss2"][:, :], d3["oss"][:, :], 1.0 / 128.0, 1e-5, ALU.mult, ALU.add),
                    reads=[f"oss_{p3}"], writes=[f"oss2_{p3}"], tag=f"oeps{h}")
                emit_rsqrt(d3["oss2"][:, :], f"oss2_{p3}", d3["rms_b"][:, :], f"rms_b_{p3}",
                           d3["nwa"][:, :], f"nwa_{p3}", d3["nwb"][:, :], f"nwb_{p3}", f"orsq{h}")
                vop(lambda e, h=h, d3=d3: e.tensor_scalar(o_raw[h][:, :], o_raw[h][:, :], d3["rms_b"][:, :], None, ALU.mult),
                    reads=[f"o_raw{h}", f"rms_b_{p3}"], writes=[f"o_raw{h}"], tag=f"onrm{h}")
            def post_ot(h):
                d3 = inst[h % 3]
                p3 = d3["_p"]
                bko = [0, 2, 1, 3][h % 4]
                tr(bko, PB[bko][:, 0:128], o_raw[h][:, :], reads=[f"o_raw{h}"], tag=f"oT{h}")
                sop(lambda e, bko=bko, d3=d3: e.copy(d3["ofm"][:, :], PB[bko][:, 0:128]),
                    reads=[f"pb{bko}"], writes=[f"ofm_{p3}"], tag=f"oTe{h}")
            def post_ao(h):
                d3 = inst[h % 3]
                p3 = d3["_p"]
                for half in range(2):
                    ob = AOB[half]
                    mm(ob, PB[ob][:, 0:512], d3["ofm"][:, :], wo_sb[h][:, half * 512:(half + 1) * 512],
                       reads=[f"ofm_{p3}", f"wo{h}"], start=(h == 0), stop=(h == H - 1), tag=f"ao{h}_{half}")
            post_chain(0); post_chain(1); post_chain(2)
            for h in range(H):
                if h + 3 < H:
                    post_chain(h + 3)
                post_ot(h)
                if h >= 1:
                    post_ao(h - 1)
            post_ao(H - 1)
            for half in range(2):
                ob = AOB[half]
                def e(eng, g=g, half=half, ob=ob):
                    return eng.tensor_add(x_tok[g][:, half * 512:(half + 1) * 512],
                                          x_tok[g][:, half * 512:(half + 1) * 512],
                                          PB[ob][:, 0:512])
                vop(e, reads=[f"pb{ob}", f"x_tok{g}"], writes=[f"x_tok{g}"], tag=f"res1{g}{half}")
        for g in range(G):
            ln_token_major(g, t)
            for c in range(FC):
                bk = rot("tp")
                tr(bk, PB[bk][:, 0:128], lnt[g][:, c * 128:(c + 1) * 128],
                   reads=[f"lnt{g}"], tag=f"ln2T{t}{g}{c}")
                def e(eng, c=c, g=g, bk=bk):
                    return eng.copy(ln2_fm[c][:, g * 128:(g + 1) * 128], PB[bk][:, 0:128])
                sop(e, reads=[f"pb{bk}"], writes=[f"lnf{c}"], tag=f"ln2E{t}{g}{c}")
        for g in range(G):
            bk = rot("small")
            for ic in range(FC):
                mm(bk, PB[bk][:, 0:NE], ln2_fm[ic][:, g * 128:(g + 1) * 128],
                   sel_sb[:, ic * NE:(ic + 1) * NE],
                   reads=[f"lnf{ic}", "sel_sb"], start=(ic == 0), stop=(ic == FC - 1), tag=f"lg{g}{ic}")
            def e(eng, g=g, bk=bk):
                return eng.activation(sgate[g][:, :], PB[bk][:, 0:NE], AF.Sigmoid)
            sop(e, reads=[f"pb{bk}"], writes=[f"sgate{g}"], tag=f"sgS{g}")
            vop(lambda e, g=g: e.max(out=top8[:, :], in_=sgate[g][:, :]),
                reads=[f"sgate{g}"], writes=["top8"], tag=f"top8{g}")
            def e2(eng, g=g):
                return eng.tensor_scalar(gmask[:, :], sgate[g][:, :], top8[:, 3:4], None, ALU.is_ge)
            vop(e2, reads=[f"sgate{g}", "top8"], writes=["gmask"], tag=f"gmask{g}")
            def e3(eng, g=g):
                return eng.tensor_mul(sgate[g][:, :], sgate[g][:, :], gmask[:, :])
            vop(e3, reads=[f"sgate{g}", "gmask"], writes=[f"sgate{g}"], tag=f"gmul{g}")
            bk2 = rot("tp")
            tr(bk2, PB[bk2][0:64, 0:128], sgate[g][:, 0:NE], reads=[f"sgate{g}"], tag=f"gT{g}")
            def e4(eng, g=g, bk2=bk2):
                return eng.copy(g_fm[:, g * 128:(g + 1) * 128], PB[bk2][0:64, 0:128])
            sop(e4, reads=[f"pb{bk2}"], writes=["g_fm"], tag=f"gE{g}")
        dma(gdram, g_fm[:, :], "gdram", reads=["g_fm"], writes=["gdram"], tag=f"gW{t}")
        for e_i in range(NE):
            sl = e_i % 2
            if e_i % 2 == 0:
                gsl = (e_i // 2) % 2
                gsrc = AP(tensor=gdram.tensor, offset=e_i * TILE, ap=[[1, 1], [1, 2 * TILE]])
                dma(g8[gsl][0:1, :], gsrc, f"g8_{gsl}", reads=["gdram"],
                    writes=[f"g8_{gsl}"], tag=f"g8L{e_i}")
            ksrc = AP(tensor=keys_d.tensor, offset=e_i * D * 128,
                      ap=[[128, 128], [128 * 128, FC], [1, 128]])
            dma(ke_sb[sl][:, :], ksrc, f"ke{sl}", writes=[f"ke{sl}"], tag=f"keL{e_i}")
            dma(ve_sb[sl][:, :], values_d[e_i, :, :], f"ve{sl}", writes=[f"ve{sl}"], tag=f"veL{e_i}")
            hbk = [2, 3][e_i % 2]
            for ic in range(FC):
                mm(hbk, PB[hbk][:, 0:TILE], ke_sb[sl][:, ic * 128:(ic + 1) * 128],
                   ln2_fm[ic][:, :],
                   reads=[f"ke{sl}", f"lnf{ic}"], start=(ic == 0), stop=(ic == FC - 1),
                   tag=f"h{e_i}i{ic}")
            if e_i % 2 == 0:
                gbk = [6, 7][(e_i // 2) % 2]
                gsl = (e_i // 2) % 2
                gpr = (e_i // 2) % 2
                mm(gbk, PB[gbk][:, 0:2 * TILE], consts["ones_row"][:, :],
                   g8[gsl][0:1, 0:2 * TILE],
                   reads=["ones_row_sb", f"g8_{gsl}"], tag=f"gb{e_i}")
                def eg(eng, gpr=gpr, gbk=gbk):
                    return eng.copy(gb_sb[gpr][:, :], PB[gbk][:, 0:2 * TILE])
                sop(eg, reads=[f"pb{gbk}"], writes=[f"gb{gpr}"], tag=f"gbE{e_i}")
            gpr = (e_i // 2) % 2
            def es(eng, sl=sl, hbk=hbk, gpr=gpr, e_i=e_i):
                return eng.scalar_tensor_tensor(sc_sb[sl][:, :], PB[hbk][:, 0:TILE], 0.0,
                                                gb_sb[gpr][:, (e_i % 2) * TILE:(e_i % 2 + 1) * TILE],
                                                ALU.max, ALU.mult)
            vop(es, reads=[f"pb{hbk}", f"gb{gpr}"], writes=[f"sc{sl}"], tag=f"sc{e_i}")
            def emit_out2(e_j):
                slj = e_j % 2
                for g in range(G):
                    for half in range(2):
                        ob = OUT2B[(g, half)]
                        mm(ob, PB[ob][:, 0:512], sc_sb[slj][:, g * 128:(g + 1) * 128],
                           ve_sb[slj][:, half * 512:(half + 1) * 512],
                           reads=[f"sc{slj}", f"ve{slj}"], start=(e_j == 0), stop=(e_j == NE - 1),
                           tag=f"o2_{e_j}g{g}h{half}")
            if e_i >= 1:
                emit_out2(e_i - 1)
            if e_i == NE - 1:
                emit_out2(e_i)
        for g in range(G):
            for half in range(2):
                ob = OUT2B[(g, half)]
                def e(eng, g=g, half=half, ob=ob):
                    return eng.tensor_add(x_tok[g][:, half * 512:(half + 1) * 512],
                                          x_tok[g][:, half * 512:(half + 1) * 512],
                                          PB[ob][:, 0:512])
                vop(e, reads=[f"pb{ob}", f"x_tok{g}"], writes=[f"x_tok{g}"], tag=f"res2{g}{half}")
            r0 = t * TILE + g * 128
            dma(out_d[r0:r0 + 128, :], x_tok[g][:, :], "out",
                reads=[f"x_tok{g}"], tag=f"st{t}{g}")

    return nc, B, ctx




_CACHE = {}

def _get_program():
    if "nc" in _CACHE:
        return _CACHE["nc"]
    nc, B, ctx = build(NT=16)
    B.finalize(nc, None)
    sems = {}
    for e in ENGINES:
        sems[("eng", e)] = ctx.enter_context(nc.semaphore(f"sem_{e}"))
    for s in B.stream_totals:
        sems[("dma", s)] = ctx.enter_context(nc.semaphore(f"dma_{s}"))
    with nc.Block() as block:
        @block.sync
        def _(eng):
            B.emit_engine("sync", eng, sems)
            for s, c in B.stream_totals.items():
                eng.wait_ge(sems[("dma", s)], 16 * c)
        @block.tensor
        def _(eng):
            B.emit_engine("tensor", eng, sems)
        @block.vector
        def _(eng):
            B.emit_engine("vector", eng, sems)
        @block.scalar
        def _(eng):
            B.emit_engine("scalar", eng, sems)
        @block.gpsimd
        def _(eng):
            B.emit_engine("gpsimd", eng, sems)
    ctx.close()
    _CACHE["nc"] = nc
    return nc


def kernel(**inputs):
    from concourse.bass_utils import run_bass_kernel_spmd
    nc = _get_program()
    in_maps = [host_inputs(inputs, core) for core in range(8)]
    res = run_bass_kernel_spmd(nc, in_maps, core_ids=list(range(8)))
    out = np.concatenate([res.results[c]["out"].reshape(2, 2048, 1024) for c in range(8)], axis=0)
    return out
